# version 35
# baseline (speedup 1.0000x reference)
import numpy as np
import concourse.bass as bass
import concourse.mybir as mybir
from concourse.bass_utils import run_bass_kernel_spmd

F32 = mybir.dt.float32
BF16 = mybir.dt.bfloat16
I32 = mybir.dt.int32
AF = mybir.ActivationFunctionType
ALU = mybir.AluOpType
AX = mybir.AxisListType

P = 128
D = 1024
KD = 64
II = 2816
S = 2048
BPC = 4
NT = S // P
NCORES = 8
EPS = 1e-5
ATT_SCALE = KD ** -0.5
NEG = -1.0e9
IG = II // 2
ECH = D // P
WCH = II // P
GCH = [(0, 512), (512, 512), (1024, 384)]

UNITS = ([("wg", r) for r in range(WCH)] + [("wu", r) for r in range(WCH)]
         + [("wd", (r, g)) for r in range(ECH) for g in range(2)])
NUW = (len(UNITS) + NCORES - 1) // NCORES

ENGINES = ("sync", "vector", "scalar", "tensor", "gpsimd")


class Buf:
    __slots__ = ("name", "w", "rs")

    def __init__(self, name):
        self.name = name
        self.w = None
        self.rs = []


class Plan:
    def __init__(self):
        self.items = {e: [] for e in ENGINES}

    def op(self, eng, fn, reads=(), writes=(), deps=(), reg_reads=()):
        dd = []
        for b in reads:
            if b.w is not None:
                dd.append(b.w)
        for b in writes:
            dd.extend(b.rs)
            if b.w is not None:
                dd.append(b.w)
        dd.extend(deps)
        tok = (eng, len(self.items[eng]))
        self.items[eng].append((fn, dd))
        for b in reads:
            b.rs.append(tok)
        for b in reg_reads:
            b.rs.append(tok)
        for b in writes:
            b.w = tok
            b.rs = []
        return tok

    def emit(self, nc):
        marked = set()
        for e in ENGINES:
            for (_, dd) in self.items[e]:
                marked.update(dd)
        for i in range(len(self.items["sync"])):
            marked.add(("sync", i))
        incidx = {}
        ninc = {}
        for e in ENGINES:
            c = 0
            for i in range(len(self.items[e])):
                if (e, i) in marked:
                    c += 1
                    incidx[(e, i)] = c
            ninc[e] = c
        sems = {e: nc.alloc_semaphore(f"sem_{e}") for e in ENGINES if ninc[e]}
        plan = self

        with nc.Block() as block:
            def section(eng):
                def body(engine):
                    water = {e: 0 for e in ENGINES}
                    for i, (fn, dd) in enumerate(plan.items[eng]):
                        for (de, di) in dd:
                            if de == eng and eng == "tensor":
                                continue
                            tgt = incidx.get((de, di))
                            if tgt is None:
                                continue
                            val = tgt * (16 if de == "sync" else 1)
                            if water[de] >= val:
                                continue
                            engine.wait_ge(sems[de], val)
                            water[de] = val
                        inst = fn(engine)
                        if (eng, i) in marked:
                            inst.then_inc(sems[eng], 16 if eng == "sync" else 1)
                    if eng == "sync" and ninc["sync"]:
                        engine.wait_ge(sems["sync"], ninc["sync"] * 16)
                return body

            for eng in ENGINES:
                getattr(block, eng)(section(eng))


def build_program(debug=False):
    nc = bass.Bass("TRN2", target_bir_lowering=False,
                   dynamic_dma_scratch_size=4096)
    pl = Plan()
    dbg = {}
    if debug:
        for nm, shp, dt_ in [("dxn", [P, D], F32), ("dxq", [P, D], F32),
                             ("dgs", [P, IG], F32), ("dh", [P, IG], F32),
                             ("dhq", [P, IG], F32), ("dqT", [KD, P], F32),
                             ("dkc", [KD, P], F32), ("dvc", [P, KD], F32),
                             ("dpb", [P, S], F32), ("doT", [KD, P], F32),
                             ("dxqT", [P, D], F32), ("dhqT", [P, II], F32),
                             ("dxb", [P, D], F32)]:
            dbg[nm] = nc.dram_tensor(nm, shp, dt_, kind="ExternalOutput").ap()

    x_d = nc.dram_tensor("x", [BPC, S, D], F32, kind="ExternalInput").ap()
    wq_d = nc.dram_tensor("wq", [D, KD], F32, kind="ExternalInput").ap()
    wk_d = nc.dram_tensor("wk", [D, KD], F32, kind="ExternalInput").ap()
    wv1_d = nc.dram_tensor("wv1", [D, KD], F32, kind="ExternalInput").ap()
    wv2_d = nc.dram_tensor("wv2", [KD, D], F32, kind="ExternalInput").ap()
    wsl_d = nc.dram_tensor("wsl", [NUW, P, IG], F32, kind="ExternalInput").ap()
    out_d = nc.dram_tensor("out", [BPC, S, D], F32, kind="ExternalOutput").ap()

    sbt = nc.alloc_sbuf_tensor
    wqb = sbt("wqb", [P, ECH * KD], BF16)
    wkb = sbt("wkb", [P, ECH * KD], BF16)
    wv1b = sbt("wv1b", [P, ECH * KD], BF16)
    wv2b = sbt("wv2b", [KD, D], BF16)
    wgT = sbt("wgT", [P, ECH * II], BF16)
    wuT = sbt("wuT", [P, ECH * II], BF16)
    wdT = sbt("wdT", [P, WCH * D], BF16)

    xb = [sbt("xb0", [P, D], F32), sbt("xb1", [P, D], F32)]
    xn = sbt("xn", [P, D], F32)
    xnT = sbt("xnT", [P, D], BF16)
    xqT = sbt("xqT", [P, D], BF16)
    p_sb = sbt("p_sb", [P, S], F32)
    p_bf = sbt("p_bf", [P, S], BF16)
    pt_sb = sbt("pt_sb", [P, 512], BF16)
    kcache = sbt("kcache", [KD, S], BF16)
    vcache = sbt("vcache", [P, NT * KD], BF16)
    qT_sb = sbt("qT_sb", [KD, P], BF16)
    oT_sb = sbt("oT_sb", [KD, P], BF16)
    hqT = sbt("hqT", [P, II], BF16)
    outsb = [sbt("outsb0", [P, D], F32)]

    PO = [sbt(f"po{i}", [P, IG], F32) for i in range(6)]

    ii_t = PO[0]
    iif = PO[1]
    idf = sbt("idf", [P, P], F32)
    idb = sbt("idb", [P, P], BF16)
    cm = sbt("cm", [P, P], F32)

    bm_t = sbt("bm_t", [P, 96], F32)
    sc_t = sbt("sc_t", [P, 96], F32)
    iv_t = sbt("iv_t", [P, 96], F32)
    ln_t = sbt("ln_t", [P, 16], F32)
    sm_t = sbt("sm_t", [P, 12], F32)

    out_ps = nc.alloc_psum_tensor("out_ps", [P, D], F32)
    psA = nc.alloc_psum_tensor("psA", [P, 1536], F32)
    psB = nc.alloc_psum_tensor("psB", [P, D], F32)
    pt_ps = nc.alloc_psum_tensor("pt_ps", [P, 512], BF16)

    print(f"[build] sbuf bytes remaining/partition: {nc.sbuf_bytes_remaining}")
    print(f"[build] psum banks remaining: {nc.psum_banks_remaining}")

    b_wqb, b_wkb, b_wv1b, b_wv2b = Buf("wqb"), Buf("wkb"), Buf("wv1b"), Buf("wv2b")
    b_wgT, b_wuT, b_wdT = Buf("wgT"), Buf("wuT"), Buf("wdT")
    b_xb = [Buf("xb0"), Buf("xb1")]
    b_xn, b_xnT, b_xqT = Buf("xn"), Buf("xnT"), Buf("xqT")
    b_psb, b_pbf = Buf("p_sb"), Buf("p_bf")
    b_pts = [Buf(f"pt_ps{i}") for i in range(4)]
    b_ptsb = [Buf(f"pt_sb{i}") for i in range(4)]
    b_kc, b_vc, b_qT, b_oT = Buf("kc"), Buf("vc"), Buf("qT"), Buf("oT")
    b_hqT = Buf("hqT")
    b_outsb = [Buf("outsb0")]
    b_po = [Buf(f"po{i}") for i in range(6)]
    b_idf, b_idb, b_cm = Buf("idf"), Buf("idb"), Buf("cm")
    b_bm, b_sc, b_iv, b_ln, b_sm = Buf("bm"), Buf("sc"), Buf("iv"), Buf("ln"), Buf("sm")
    b_out0, b_out1 = Buf("out0"), Buf("out1")
    b_A = [Buf("A0"), Buf("A1"), Buf("A2")]
    b_B = [Buf("B0"), Buf("B1")]

    dma = lambda fn, reads=(), writes=(), deps=(), reg_reads=(): pl.op(
        "sync", fn, reads, writes, deps, reg_reads)
    vop = lambda fn, reads=(), writes=(), deps=(), reg_reads=(): pl.op(
        "vector", fn, reads, writes, deps, reg_reads)
    aop = lambda fn, reads=(), writes=(), deps=(), reg_reads=(): pl.op(
        "scalar", fn, reads, writes, deps, reg_reads)
    top = lambda fn, reads=(), writes=(), deps=(), reg_reads=(): pl.op(
        "tensor", fn, reads, writes, deps, reg_reads)
    gop = lambda fn, reads=(), writes=(), deps=(): pl.op("gpsimd", fn, reads, writes, deps)

    gop(lambda g: g.iota(ii_t[:, 0:P].bitcast(I32), [[1, P]], channel_multiplier=-1),
        writes=[b_po[0]])
    vop(lambda v: v.tensor_copy(iif[:, 0:P], ii_t[:, 0:P].bitcast(I32)),
        reads=[b_po[0]], writes=[b_po[1]])
    vop(lambda v: v.tensor_scalar(idf[:], iif[:, 0:P], 0.0, None, ALU.is_equal),
        reads=[b_po[1]], writes=[b_idf])
    vop(lambda v: v.tensor_copy(idb[:], idf[:]), reads=[b_idf], writes=[b_idb])
    vop(lambda v: v.tensor_scalar(cm[:], iif[:, 0:P], 0.0, NEG, ALU.is_gt, ALU.mult),
        reads=[b_po[1]], writes=[b_cm])

    def attw(dst_sb, b_dst, src_d):
        dma(lambda s: s.dma_start(
            p_sb[:, 0:ECH * KD].rearrange("p (c k) -> p c k", k=KD),
            src_d.rearrange("(c p) k -> p c k", p=P)), writes=[b_psb])
        vop(lambda v: v.tensor_copy(dst_sb[:], p_sb[:, 0:ECH * KD]),
            reads=[b_psb], writes=[b_dst])

    attw(wqb, b_wqb, wq_d)
    attw(wkb, b_wkb, wk_d)
    attw(wv1b, b_wv1b, wv1_d)
    dma(lambda s: s.dma_start(p_sb[0:KD, 0:D], wv2_d), writes=[b_psb])
    vop(lambda v: v.tensor_copy(wv2b[:], p_sb[0:KD, 0:D]),
        reads=[b_psb], writes=[b_wv2b])

    def bc16(t, b0, nblk):
        return bass.AP(t, b0, [[96, P], [1, nblk], [0, 16]])

    def quant(src_sb, src_off, b_src, F, dst_sb, dst_off, b_dst, mid_hook=None,
              abs_sign_done=False, chunks=None, chunk_hook=None):
        nblk = F // 16
        s_ = src_sb[:, src_off:src_off + F]
        pA, pB, pC, pD, pE = PO[1], PO[2], PO[3], PO[4], PO[5]
        bA, bB, bC, bD, bE = b_po[1], b_po[2], b_po[3], b_po[4], b_po[5]
        vop(lambda v: v.tensor_reduce(
            bm_t[:, 0:nblk], s_.rearrange("p (b k) -> p b k", k=16),
            AX.X, ALU.max, apply_absolute_value=True),
            reads=[b_src], writes=[b_bm])
        vop(lambda v: v.tensor_scalar(sc_t[:, 0:nblk], bm_t[:, 0:nblk],
                                      1.0 / 6.0, 1e-12, ALU.mult, ALU.max),
            reads=[b_bm], writes=[b_sc])
        vop(lambda v: v.reciprocal(iv_t[:, 0:nblk], sc_t[:, 0:nblk]),
            reads=[b_sc], writes=[b_iv])
        if not abs_sign_done:
            aop(lambda a: a.activation(pA[:, 0:F], s_, AF.Abs),
                reads=[b_src], writes=[bA])
            aop(lambda a: a.activation(pB[:, 0:F], s_, AF.Sign),
                reads=[b_src], writes=[bB])
        if chunks is None:
            chunks = [(0, F)]
        dd_pre = list(b_dst.rs) + ([b_dst.w] if b_dst.w is not None else [])
        b_dst.rs = []
        last_tok = None
        for ci, (c0, cw) in enumerate(chunks):
            b0, nb = c0 // 16, cw // 16
            cs = slice(c0, c0 + cw)
            vop(lambda v, cs=cs, b0=b0, nb=nb: v.tensor_tensor(
                pC[:, cs].rearrange("p (b k) -> p b k", k=16),
                pA[:, cs].rearrange("p (b k) -> p b k", k=16),
                bc16(iv_t, b0, nb), ALU.mult),
                reads=[bA, b_iv], writes=[bC])
            if ci == 0 and mid_hook is not None:
                mid_hook()
            vop(lambda v, cs=cs: v.tensor_scalar(pE[:, cs], pC[:, cs], 2.5,
                                                 None, ALU.is_gt),
                reads=[bC], writes=[bE])
            vop(lambda v, cs=cs: v.tensor_scalar(pA[:, cs], pC[:, cs], 0.25,
                                                 None, ALU.is_gt),
                reads=[bC], writes=[bA])
            vop(lambda v, cs=cs: v.scalar_tensor_tensor(
                pD[:, cs], pC[:, cs], 0.75, pA[:, cs], ALU.is_gt, ALU.add),
                reads=[bC, bA], writes=[bD])
            vop(lambda v, cs=cs: v.scalar_tensor_tensor(
                pA[:, cs], pC[:, cs], 1.25, pD[:, cs], ALU.is_gt, ALU.add),
                reads=[bC, bD], writes=[bA])
            vop(lambda v, cs=cs: v.scalar_tensor_tensor(
                pD[:, cs], pC[:, cs], 1.75, pA[:, cs], ALU.is_gt, ALU.add),
                reads=[bC, bA], writes=[bD])
            vop(lambda v, cs=cs: v.scalar_tensor_tensor(
                pA[:, cs], pC[:, cs], 3.5, pE[:, cs], ALU.is_gt, ALU.add),
                reads=[bC, bE], writes=[bA])
            vop(lambda v, cs=cs: v.tensor_scalar(pE[:, cs], pC[:, cs], 5.0,
                                                 None, ALU.is_gt),
                reads=[bC], writes=[bE])
            vop(lambda v, cs=cs: v.scalar_tensor_tensor(
                pC[:, cs], pD[:, cs], 0.5, pA[:, cs], ALU.mult, ALU.add),
                reads=[bD, bA], writes=[bC])
            vop(lambda v, cs=cs: v.scalar_tensor_tensor(
                pA[:, cs], pE[:, cs], 2.0, pC[:, cs], ALU.mult, ALU.add),
                reads=[bE, bC], writes=[bA])
            vop(lambda v, cs=cs, b0=b0, nb=nb: v.tensor_tensor(
                pD[:, cs].rearrange("p (b k) -> p b k", k=16),
                pB[:, cs].rearrange("p (b k) -> p b k", k=16),
                bc16(sc_t, b0, nb), ALU.mult),
                reads=[bB, b_sc], writes=[bD])
            ck = Buf(f"qck{ci}")
            last_tok = vop(lambda v, cs=cs, c0=c0, cw=cw: v.tensor_tensor(
                dst_sb[:, dst_off + c0:dst_off + c0 + cw],
                pA[:, cs], pD[:, cs], ALU.mult),
                reads=[bA, bD], writes=[ck], deps=dd_pre)
            if chunk_hook is not None:
                chunk_hook(ci, c0, cw, ck)
        b_dst.w = last_tok

    contrib = nc.dram_tensor("contrib", [NUW * P, IG], BF16, kind="Internal")
    gath = nc.dram_tensor("gath", [NCORES * NUW * P, IG], BF16, kind="Internal")
    b_contrib, b_gath = Buf("contrib"), Buf("gath")

    def wp_hook(ci, c0, cw, ck):
        u0, nu = c0 // P, cw // P
        poff = 512 if ci == 1 else 0
        bw = b_B[1] if ci == 1 else b_B[0]
        for u in range(u0, u0 + nu):
            top(lambda t_, u=u, u0=u0, poff=poff: t_.transpose(
                psB[:, poff + (u - u0) * P:poff + (u - u0 + 1) * P],
                PO[0][:, u * P:(u + 1) * P], idf[:]),
                reads=[ck, b_idf], writes=[bw], reg_reads=[b_po[0]])
        aop(lambda a, u0=u0, nu=nu, poff=poff: a.activation(
            p_bf[:, u0 * P:(u0 + nu) * P], psB[:, poff:poff + nu * P], AF.Copy),
            reads=[bw], writes=[b_pbf])

    for k in range(NUW):
        dma(lambda s, k=k: s.dma_start(p_sb[:, 0:IG], wsl_d[k]), writes=[b_psb])
        quant(p_sb, 0, b_psb, IG, PO[0], 0, b_po[0], chunks=GCH,
              chunk_hook=wp_hook)
        dma(lambda s, k=k: s.dma_start(contrib[k * P:(k + 1) * P, :],
                                       p_bf[:, 0:IG]),
            reads=[b_pbf], writes=[b_contrib] if k == NUW - 1 else [])

    gop(lambda g: g.collective_compute(
        "AllGather", ALU.bypass, replica_groups=[list(range(NCORES))],
        ins=[contrib.ap().opt()], outs=[gath.ap().opt()]),
        reads=[b_contrib], writes=[b_gath])

    last_gu = {}
    for gu in range(len(UNITS)):
        last_gu[UNITS[gu][0]] = gu
    for gu in range(len(UNITS)):
        kind, args = UNITS[gu]
        row = (gu % NCORES) * (NUW * P) + (gu // NCORES) * P
        if kind == "wg" or kind == "wu":
            r = args
            dstT, b_dstT = (wgT, b_wgT) if kind == "wg" else (wuT, b_wuT)
            dma(lambda s, row=row, r=r, dstT=dstT: s.dma_start(
                bass.AP(dstT, r * P, [[ECH * II, P], [II, ECH], [1, P]]),
                gath[row:row + P, 0:D].rearrange("p (e j) -> p e j", j=P)),
                reads=[b_gath],
                writes=[b_dstT] if gu == last_gu[kind] else [])
        else:
            r, g = args
            dma(lambda s, row=row, r=r, g=g: s.dma_start(
                bass.AP(wdT, (g * 11) * D + r * P, [[WCH * D, P], [D, 11], [1, P]]),
                gath[row:row + P, :].rearrange("p (u j) -> p u j", j=P)),
                reads=[b_gath],
                writes=[b_wdT] if gu == last_gu[kind] else [])

    def combine(op, slots, wa, wb):
        cur = slots[0]
        work = [wa, wb]
        for idx, s in enumerate(slots[1:]):
            dst = work[idx % 2]
            vop(lambda v, dst=dst, cur=cur, s=s: v.tensor_tensor(
                sm_t[:, dst:dst + 1], sm_t[:, cur:cur + 1], sm_t[:, s:s + 1], op),
                reads=[b_sm], writes=[b_sm])
            cur = dst
        return cur

    dma(lambda s: s.dma_start(xb[0][:], x_d[0, 0:P, :]), writes=[b_xb[0]])

    def plan_resid(rb, rt, rxc, rb_xc):
        ob, b_ob = outsb[0], b_outsb[0]
        vop(lambda v, ob=ob, xc=rxc: v.tensor_tensor(ob[:, 0:512], xc[:, 0:512],
                                                     out_ps[:, 0:512], ALU.add),
            reads=[rb_xc, b_out0], writes=[b_ob])
        vop(lambda v, ob=ob, xc=rxc: v.tensor_tensor(ob[:, 512:1024], xc[:, 512:1024],
                                                     out_ps[:, 512:1024], ALU.add),
            reads=[rb_xc, b_out1], writes=[b_ob])
        dma(lambda s, b=rb, t=rt, ob=ob: s.dma_start(
            out_d[b, t * P:(t + 1) * P, :], ob[:]), reads=[b_ob])

    pending_resid = None

    for tile in range(BPC * NT):
        b, t = divmod(tile, NT)
        xc, b_xc = xb[tile % 2], b_xb[tile % 2]

        vop(lambda v, xc=xc: v.bn_stats(ln_t[:, 0:6], xc[:, 0:512]),
            reads=[b_xc], writes=[b_ln])
        vop(lambda v, xc=xc: v.bn_stats(ln_t[:, 6:12], xc[:, 512:1024]),
            reads=[b_xc], writes=[b_ln])
        vop(lambda v: v.bn_aggr(ln_t[:, 12:14], ln_t[:, 0:12]),
            reads=[b_ln], writes=[b_ln])
        vop(lambda v: v.tensor_scalar(ln_t[:, 14:15], ln_t[:, 13:14], 1.0, EPS,
                                      ALU.mult, ALU.add),
            reads=[b_ln], writes=[b_ln])
        aop(lambda a: a.activation(ln_t[:, 15:16], ln_t[:, 14:15], AF.Ln),
            reads=[b_ln], writes=[b_ln])
        aop(lambda a: a.activation(ln_t[:, 15:16], ln_t[:, 15:16], AF.Exp, scale=-0.5),
            reads=[b_ln], writes=[b_ln])
        vop(lambda v: v.scalar_tensor_tensor(ln_t[:, 14:15], ln_t[:, 12:13], -1.0,
                                             ln_t[:, 15:16], ALU.mult, ALU.mult),
            reads=[b_ln], writes=[b_ln])
        vop(lambda v, xc=xc: v.tensor_scalar(xn[:], xc[:], ln_t[:, 15:16], ln_t[:, 14:15],
                                             ALU.mult, ALU.add),
            reads=[b_xc, b_ln], writes=[b_xn])
        aop(lambda a: a.activation(PO[1][:, 0:D], xn[:], AF.Abs),
            reads=[b_xn], writes=[b_po[1]])
        aop(lambda a: a.activation(PO[2][:, 0:D], xn[:], AF.Sign),
            reads=[b_xn], writes=[b_po[2]])

        def dbg_dump(name, ap_fn, rbuf, cast_from=None, cast_shape=None):
            if not (debug and tile == 0):
                return
            if cast_from is not None:
                vop(lambda v: v.tensor_copy(PO[1][0:cast_shape[0], 0:cast_shape[1]],
                                            cast_from), reads=[rbuf], writes=[b_po[1]])
                dma(lambda s: s.dma_start(dbg[name],
                                          PO[1][0:cast_shape[0], 0:cast_shape[1]]),
                    reads=[b_po[1]])
            else:
                dma(lambda s: s.dma_start(dbg[name], ap_fn), reads=[rbuf])

        dbg_dump("dxb", xc[:], b_xc)
        dbg_dump("dxn", xn[:], b_xn)

        if pending_resid is not None:
            plan_resid(*pending_resid)
            pending_resid = None
        if tile + 1 < BPC * NT:
            nb, ntt = divmod(tile + 1, NT)
            xnx, b_xnx = xb[(tile + 1) % 2], b_xb[(tile + 1) % 2]
            dma(lambda s, nb=nb, ntt=ntt, xnx=xnx: s.dma_start(
                xnx[:], x_d[nb, ntt * P:(ntt + 1) * P, :]), writes=[b_xnx])


        for e in range(ECH):
            top(lambda t_, e=e: t_.transpose(psB[:, e * P:(e + 1) * P],
                                             xn[:, e * P:(e + 1) * P], idf[:]),
                reads=[b_xn, b_idf], writes=[b_B[e // 4]])
        aop(lambda a: a.activation(xnT[:], psB[:], AF.Copy),
            reads=[b_B[0], b_B[1]], writes=[b_xnT])

        for e in range(ECH):
            top(lambda t_, e=e: t_.matmul(psA[0:KD, 0:P],
                                          wqb[:, e * KD:(e + 1) * KD],
                                          xnT[:, e * P:(e + 1) * P],
                                          start=(e == 0), stop=(e == ECH - 1)),
                reads=[b_wqb, b_xnT], writes=[b_A[0]])
        aop(lambda a: a.activation(qT_sb[:], psA[0:KD, 0:P], AF.Copy),
            reads=[b_A[0]], writes=[b_qT])
        dbg_dump("dqT", None, b_qT, cast_from=qT_sb[:], cast_shape=(KD, P))
        for e in range(ECH):
            top(lambda t_, e=e: t_.matmul(psA[0:KD, 512:512 + P],
                                          wkb[:, e * KD:(e + 1) * KD],
                                          xnT[:, e * P:(e + 1) * P],
                                          start=(e == 0), stop=(e == ECH - 1)),
                reads=[b_wkb, b_xnT], writes=[b_A[1]])
        aop(lambda a, t=t: a.activation(kcache[:, t * P:(t + 1) * P],
                                        psA[0:KD, 512:512 + P], AF.Copy),
            reads=[b_A[1]], writes=[b_kc])
        dbg_dump("dkc", None, b_kc, cast_from=kcache[:, 0:P], cast_shape=(KD, P))
        for e in range(ECH):
            top(lambda t_, e=e: t_.matmul(psA[:, 1024:1024 + KD],
                                          xnT[:, e * P:(e + 1) * P],
                                          wv1b[:, e * KD:(e + 1) * KD],
                                          start=(e == 0), stop=(e == ECH - 1)),
                reads=[b_wv1b, b_xnT], writes=[b_A[2]])
        aop(lambda a, t=t: a.activation(vcache[:, t * KD:(t + 1) * KD],
                                        psA[:, 1024:1024 + KD], AF.Copy),
            reads=[b_A[2]], writes=[b_vc])
        dbg_dump("dvc", None, b_vc, cast_from=vcache[:, 0:KD], cast_shape=(P, KD))

        L = (t + 1) * P
        nch = (L + 511) // 512

        def xq_hook(ci, c0, cw, ck):
            for e in range(c0 // P, (c0 + cw) // P):
                top(lambda t_, e=e: t_.transpose(psB[:, e * P:(e + 1) * P],
                                                 PO[0][:, e * P:(e + 1) * P], idf[:]),
                    reads=[ck, b_idf], writes=[b_B[e // 4]], reg_reads=[b_po[0]])
            aop(lambda a, c0=c0, cw=cw, ci=ci: a.activation(
                xqT[:, c0:c0 + cw], psB[:, c0:c0 + cw], AF.Copy),
                reads=[b_B[ci]], writes=[b_xqT])

        chunks = []

        def plan_scores():
            for c in range(nch):
                n = min(512, L - c * 512)
                if c < 3:
                    ap = psA[:, c * 512:c * 512 + n]
                    bb = b_A[c]
                else:
                    ap = psB[:, 0:n]
                    bb = b_B[0]
                chunks.append((c, n, ap, bb))
                top(lambda t_, ap=ap, c=c, n=n: t_.matmul(
                    ap, qT_sb[:], kcache[:, c * 512:c * 512 + n],
                    start=True, stop=True),
                    reads=[b_qT, b_kc], writes=[bb])

        def plan_softmax_front():
            cd, wo = divmod(t * P, 512)
            dap, dbb = (psA[:, cd * 512 + wo:cd * 512 + wo + P], b_A[cd]) if cd < 3 \
                else (psB[:, wo:wo + P], b_B[0])
            vop(lambda v, dap=dap: v.tensor_tensor(dap, dap, cm[:], ALU.add),
                reads=[b_cm, dbb], writes=[dbb])
            for (c, n, ap, bb) in chunks:
                vop(lambda v, ap=ap, c=c: v.tensor_reduce(
                    sm_t[:, c:c + 1], ap, AX.X, ALU.max),
                    reads=[bb], writes=[b_sm])
            rmx = combine(ALU.max, list(range(nch)), 4, 10) if nch > 1 else 0
            vop(lambda v, rmx=rmx: v.tensor_scalar(
                sm_t[:, 5:6], sm_t[:, rmx:rmx + 1], -ATT_SCALE, None, ALU.mult),
                reads=[b_sm], writes=[b_sm])
            for (c, n, ap, bb) in chunks:
                aop(lambda a, ap=ap, c=c, n=n: a.activation(
                    p_bf[:, c * 512:c * 512 + n], ap, AF.Exp,
                    scale=ATT_SCALE, bias=sm_t[:, 5:6],
                    accum_out=sm_t[:, 6 + c:7 + c]),
                    reads=[bb, b_sm], writes=[b_pbf, b_sm])

        XCH = [(0, 512), (512, 512)]
        if nch <= 3:
            plan_scores()
            quant(xn, 0, b_xn, D, PO[0], 0, b_po[0], mid_hook=plan_softmax_front,
                  abs_sign_done=True, chunks=XCH, chunk_hook=xq_hook)
        else:
            quant(xn, 0, b_xn, D, PO[0], 0, b_po[0], abs_sign_done=True,
                  chunks=XCH, chunk_hook=xq_hook)
            plan_scores()
            plan_softmax_front()
        dbg_dump("dxq", PO[0][:, 0:D], b_po[0])
        dbg_dump("dxqT", None, b_xqT, cast_from=xqT[:], cast_shape=(P, D))

        rtt = combine(ALU.add, list(range(6, 6 + nch)), 10, 4) if nch > 1 else 6
        aop(lambda a, rtt=rtt: a.activation(sm_t[:, 1:2], sm_t[:, rtt:rtt + 1], AF.Ln),
            reads=[b_sm], writes=[b_sm])
        vop(lambda v: v.tensor_tensor(sm_t[:, 0:1], sm_t[:, 5:6], sm_t[:, 1:2],
                                      ALU.subtract),
            reads=[b_sm], writes=[b_sm])
        for (c, n, ap, bb) in chunks:
            aop(lambda a, ap=ap, c=c, n=n: a.activation(
                p_bf[:, c * 512:c * 512 + n], ap, AF.Exp,
                scale=ATT_SCALE, bias=sm_t[:, 0:1]),
                reads=[bb, b_sm], writes=[b_pbf])

        def gumm(wT, bwT, g):
            for e in range(ECH):
                for (c0, cw) in GCH:
                    top(lambda t_, e=e, g=g, c0=c0, cw=cw, wT=wT: t_.matmul(
                        psA[:, c0:c0 + cw], xqT[:, e * P:(e + 1) * P],
                        wT[:, e * II + g * IG + c0:e * II + g * IG + c0 + cw],
                        start=(e == 0), stop=(e == ECH - 1)),
                        reads=[b_xqT, bwT], writes=[b_A[c0 // 512]])

        def gu_chunk(wT, bwT, g, c0, cw):
            bk = b_A[c0 // 512]
            for e in range(ECH):
                top(lambda t_, e=e, g=g, c0=c0, cw=cw, wT=wT: t_.matmul(
                    psA[:, c0:c0 + cw], xqT[:, e * P:(e + 1) * P],
                    wT[:, e * II + g * IG + c0:e * II + g * IG + c0 + cw],
                    start=(e == 0), stop=(e == ECH - 1)),
                    reads=[b_xqT, bwT], writes=[bk])
            return bk

        for (c0, cw) in GCH:
            bk = gu_chunk(wgT, b_wgT, 0, c0, cw)
            aop(lambda a, c0=c0, cw=cw: a.activation(
                PO[0][:, c0:c0 + cw], psA[:, c0:c0 + cw], AF.Silu),
                reads=[bk], writes=[b_po[0]])
        dbg_dump("dgs", PO[0][:, 0:IG], b_po[0])

        for (c0, cw) in GCH:
            bk = gu_chunk(wuT, b_wuT, 0, c0, cw)
            vop(lambda v, c0=c0, cw=cw: v.tensor_tensor(
                PO[0][:, c0:c0 + cw], PO[0][:, c0:c0 + cw], psA[:, c0:c0 + cw],
                ALU.mult),
                reads=[b_po[0], bk], writes=[b_po[0]])
        dbg_dump("dh", PO[0][:, 0:IG], b_po[0])
        aop(lambda a: a.activation(PO[1][:, 0:IG], PO[0][:, 0:IG], AF.Abs),
            reads=[b_po[0]], writes=[b_po[1]])
        aop(lambda a: a.activation(PO[2][:, 0:IG], PO[0][:, 0:IG], AF.Sign),
            reads=[b_po[0]], writes=[b_po[2]])
        for (c0, cw) in GCH:
            bk = gu_chunk(wgT, b_wgT, 1, c0, cw)
            aop(lambda a, c0=c0, cw=cw: a.activation(
                p_sb[:, c0:c0 + cw], psA[:, c0:c0 + cw], AF.Silu),
                reads=[bk], writes=[b_psb])
        for sx in range(t + 1):
            sl = sx % 4
            top(lambda t_, sx=sx, sl=sl: t_.transpose(
                pt_ps[:, sl * P:(sl + 1) * P], p_bf[:, sx * P:(sx + 1) * P], idb[:]),
                reads=[b_pbf, b_idb], writes=[b_pts[sl]])
            aop(lambda a, sl=sl: a.activation(
                pt_sb[:, sl * P:(sl + 1) * P], pt_ps[:, sl * P:(sl + 1) * P], AF.Copy),
                reads=[b_pts[sl]], writes=[b_ptsb[sl]])
            top(lambda t_, sx=sx, sl=sl, t=t: t_.matmul(
                out_ps[0:KD, 0:P], vcache[:, sx * KD:(sx + 1) * KD],
                pt_sb[:, sl * P:(sl + 1) * P],
                start=(sx == 0), stop=(sx == t)),
                reads=[b_vc, b_ptsb[sl]], writes=[b_out0])
        aop(lambda a: a.activation(oT_sb[:], out_ps[0:KD, 0:P], AF.Copy),
            reads=[b_out0], writes=[b_oT])
        dbg_dump("doT", None, b_oT, cast_from=oT_sb[:], cast_shape=(KD, P))
        top(lambda t_: t_.matmul(out_ps[:, 0:512], oT_sb[:], wv2b[:, 0:512],
                                 start=True, stop=False, skip_group_check=True),
            reads=[b_oT, b_wv2b], writes=[b_out0])
        top(lambda t_: t_.matmul(out_ps[:, 512:1024], oT_sb[:], wv2b[:, 512:1024],
                                 start=True, stop=False, skip_group_check=True),
            reads=[b_oT, b_wv2b], writes=[b_out1])

        def hq_hook(g, hs, b_hs):
            def hook(ci, c0, cw, ck):
                u0, nu = c0 // P, cw // P
                poff = 512 if ci == 1 else 0
                bw = b_B[1] if ci == 1 else b_B[0]
                for u in range(u0, u0 + nu):
                    top(lambda t_, u=u, u0=u0, poff=poff, hs=hs: t_.transpose(
                        psB[:, poff + (u - u0) * P:poff + (u - u0 + 1) * P],
                        hs[:, u * P:(u + 1) * P], idf[:]),
                        reads=[ck, b_idf], writes=[bw], reg_reads=[b_hs])
                aop(lambda a, g=g, u0=u0, nu=nu, poff=poff: a.activation(
                    hqT[:, (g * 11 + u0) * P:(g * 11 + u0 + nu) * P],
                    psB[:, poff:poff + nu * P], AF.Copy),
                    reads=[bw], writes=[b_hqT])
            return hook

        def hq_mms(g):
            for u in range(11):
                slot = g * 11 + u
                last = (g == 1 and u == 10)
                top(lambda t_, slot=slot, last=last: t_.matmul(
                    out_ps[:, 0:512], hqT[:, slot * P:(slot + 1) * P],
                    wdT[:, slot * D:slot * D + 512],
                    start=False, stop=last, skip_group_check=True),
                    reads=[b_hqT, b_wdT], writes=[b_out0])
                top(lambda t_, slot=slot, last=last: t_.matmul(
                    out_ps[:, 512:1024], hqT[:, slot * P:(slot + 1) * P],
                    wdT[:, slot * D + 512:slot * D + 1024],
                    start=False, stop=last, skip_group_check=True),
                    reads=[b_hqT, b_wdT], writes=[b_out1])

        quant(PO[0], 0, b_po[0], IG, PO[0], 0, b_po[0], abs_sign_done=True,
              chunks=GCH, chunk_hook=hq_hook(0, PO[0], b_po[0]))
        dbg_dump("dhq", PO[0][:, 0:IG], b_po[0])
        for (c0, cw) in GCH:
            bk = gu_chunk(wuT, b_wuT, 1, c0, cw)
            vop(lambda v, c0=c0, cw=cw: v.tensor_tensor(
                p_sb[:, c0:c0 + cw], p_sb[:, c0:c0 + cw], psA[:, c0:c0 + cw],
                ALU.mult),
                reads=[b_psb, bk], writes=[b_psb])

        aop(lambda a: a.activation(PO[1][:, 0:IG], p_sb[:, 0:IG], AF.Abs),
            reads=[b_psb], writes=[b_po[1]])
        aop(lambda a: a.activation(PO[2][:, 0:IG], p_sb[:, 0:IG], AF.Sign),
            reads=[b_psb], writes=[b_po[2]])
        hq_mms(0)
        quant(p_sb, 0, b_psb, IG, p_sb, 0, b_psb, abs_sign_done=True,
              chunks=GCH, chunk_hook=hq_hook(1, p_sb, b_psb))
        hq_mms(1)

        pending_resid = (b, t, xc, b_xc)

    if pending_resid is not None:
        plan_resid(*pending_resid)
        pending_resid = None

    pl.emit(nc)
    n_ops = {e: len(pl.items[e]) for e in ENGINES}
    print(f"[build] ops per engine: {n_ops}")
    return nc


def kernel(L0_attn_x, L0_attn_w_q, L0_attn_w_k, L0_attn_w_v1, L0_attn_w_v2,
           L0_ffn_w_gate, L0_ffn_w_up, L0_ffn_w_down):
    x = np.ascontiguousarray(np.asarray(L0_attn_x, dtype=np.float32))
    ws = {
        "wq": np.ascontiguousarray(np.asarray(L0_attn_w_q, dtype=np.float32)),
        "wk": np.ascontiguousarray(np.asarray(L0_attn_w_k, dtype=np.float32)),
        "wv1": np.ascontiguousarray(np.asarray(L0_attn_w_v1, dtype=np.float32)),
        "wv2": np.ascontiguousarray(np.asarray(L0_attn_w_v2, dtype=np.float32)),
        "wg": np.ascontiguousarray(np.asarray(L0_ffn_w_gate, dtype=np.float32)),
        "wu": np.ascontiguousarray(np.asarray(L0_ffn_w_up, dtype=np.float32)),
        "wd": np.ascontiguousarray(np.asarray(L0_ffn_w_down, dtype=np.float32)),
    }
    nc = build_program()
    wsls = []
    for c in range(NCORES):
        sl = np.zeros((NUW, P, IG), np.float32)
        mine = UNITS[c::NCORES]
        for k, (kind, args) in enumerate(mine):
            if kind == "wg":
                sl[k, :, 0:D] = ws["wg"][args * P:(args + 1) * P, :]
            elif kind == "wu":
                sl[k, :, 0:D] = ws["wu"][args * P:(args + 1) * P, :]
            else:
                r, g = args
                sl[k] = ws["wd"][r * P:(r + 1) * P, g * IG:(g + 1) * IG]
        wsls.append(np.ascontiguousarray(sl))
    wsm = {k: v for k, v in ws.items() if k in ("wq", "wk", "wv1", "wv2")}
    in_maps = [dict(wsm, x=np.ascontiguousarray(x[c * BPC:(c + 1) * BPC]),
                    wsl=wsls[c])
               for c in range(NCORES)]
    res = run_bass_kernel_spmd(nc, in_maps, core_ids=list(range(NCORES)))
    out = np.concatenate([np.asarray(res.results[c]["out"])
                          for c in range(NCORES)], axis=0)
    return out


if __name__ == "__main__":
    build_program()
    print("build ok")


# version 37
# speedup vs baseline: 1.1200x; 1.1200x over previous
import numpy as np
import concourse.bass as bass
import concourse.mybir as mybir
from concourse.bass_utils import run_bass_kernel_spmd

F32 = mybir.dt.float32
BF16 = mybir.dt.bfloat16
I32 = mybir.dt.int32
AF = mybir.ActivationFunctionType
ALU = mybir.AluOpType
AX = mybir.AxisListType

P = 128
D = 1024
KD = 64
II = 2816
S = 2048
BPC = 4
NT = S // P
NCORES = 8
EPS = 1e-5
ATT_SCALE = KD ** -0.5
NEG = -1.0e9
IG = II // 2
ECH = D // P
WCH = II // P
GCH = [(0, 512), (512, 512), (1024, 384)]

UNITS = ([("wg", r) for r in range(WCH)] + [("wu", r) for r in range(WCH)]
         + [("wd", (r, g)) for r in range(ECH) for g in range(2)])
NUW = (len(UNITS) + NCORES - 1) // NCORES

ENGINES = ("sync", "vector", "scalar", "tensor", "gpsimd")


class Buf:
    __slots__ = ("name", "w", "rs")

    def __init__(self, name):
        self.name = name
        self.w = None
        self.rs = []


class Plan:
    def __init__(self):
        self.items = {e: [] for e in ENGINES}

    def op(self, eng, fn, reads=(), writes=(), deps=()):
        dd = []
        for b in reads:
            if b.w is not None:
                dd.append(b.w)
        for b in writes:
            dd.extend(b.rs)
            if b.w is not None:
                dd.append(b.w)
        dd.extend(deps)
        tok = (eng, len(self.items[eng]))
        self.items[eng].append((fn, dd, len(self.items["sync"])))
        for b in reads:
            b.rs.append(tok)
        for b in writes:
            b.w = tok
            b.rs = []
        return tok

    def emit(self, nc):
        marked = set()
        for e in ENGINES:
            for (_, dd, _ns) in self.items[e]:
                marked.update(dd)
        for i in range(len(self.items["sync"])):
            marked.add(("sync", i))
        incidx = {}
        ninc = {}
        for e in ENGINES:
            c = 0
            for i in range(len(self.items[e])):
                if (e, i) in marked:
                    c += 1
                    incidx[(e, i)] = c
            ninc[e] = c
        sems = {e: nc.alloc_semaphore(f"sem_{e}") for e in ENGINES if ninc[e]}
        plan = self

        with nc.Block() as block:
            def section(eng):
                def body(engine):
                    water = {e: 0 for e in ENGINES}
                    for i, (fn, dd, nsync) in enumerate(plan.items[eng]):
                        for (de, di) in dd:
                            if de == eng and eng == "tensor":
                                continue
                            if de == "sync":
                                val = nsync * 16
                            else:
                                tgt = incidx.get((de, di))
                                if tgt is None:
                                    continue
                                val = tgt
                            if val == 0 or water[de] >= val:
                                continue
                            engine.wait_ge(sems[de], val)
                            water[de] = val
                        inst = fn(engine)
                        if (eng, i) in marked:
                            inst.then_inc(sems[eng], 16 if eng == "sync" else 1)
                    if eng == "sync" and ninc["sync"]:
                        engine.wait_ge(sems["sync"], ninc["sync"] * 16)
                return body

            for eng in ENGINES:
                getattr(block, eng)(section(eng))


def build_program(debug=False):
    nc = bass.Bass("TRN2", target_bir_lowering=False,
                   dynamic_dma_scratch_size=4096)
    pl = Plan()
    dbg = {}
    if debug:
        for nm, shp, dt_ in [("dxn", [P, D], F32), ("dxq", [P, D], F32),
                             ("dgs", [P, IG], F32), ("dh", [P, IG], F32),
                             ("dhq", [P, IG], F32), ("dqT", [KD, P], F32),
                             ("dkc", [KD, P], F32), ("dvc", [P, KD], F32),
                             ("dpb", [P, S], F32), ("doT", [KD, P], F32),
                             ("dxqT", [P, D], F32), ("dhqT", [P, II], F32),
                             ("dxb", [P, D], F32)]:
            dbg[nm] = nc.dram_tensor(nm, shp, dt_, kind="ExternalOutput").ap()

    x_d = nc.dram_tensor("x", [BPC, S, D], F32, kind="ExternalInput").ap()
    wq_d = nc.dram_tensor("wq", [D, KD], F32, kind="ExternalInput").ap()
    wk_d = nc.dram_tensor("wk", [D, KD], F32, kind="ExternalInput").ap()
    wv1_d = nc.dram_tensor("wv1", [D, KD], F32, kind="ExternalInput").ap()
    wv2_d = nc.dram_tensor("wv2", [KD, D], F32, kind="ExternalInput").ap()
    wsl_d = nc.dram_tensor("wsl", [NUW, P, IG], F32, kind="ExternalInput").ap()
    out_d = nc.dram_tensor("out", [BPC, S, D], F32, kind="ExternalOutput").ap()

    sbt = nc.alloc_sbuf_tensor
    wqb = sbt("wqb", [P, ECH * KD], BF16)
    wkb = sbt("wkb", [P, ECH * KD], BF16)
    wv1b = sbt("wv1b", [P, ECH * KD], BF16)
    wv2b = sbt("wv2b", [KD, D], BF16)
    wgT = sbt("wgT", [P, ECH * II], BF16)
    wuT = sbt("wuT", [P, ECH * II], BF16)
    wdT = sbt("wdT", [P, WCH * D], BF16)

    xb = [sbt("xb0", [P, D], F32), sbt("xb1", [P, D], F32)]
    xn = sbt("xn", [P, D], F32)
    xnT = sbt("xnT", [P, D], BF16)
    xqT = sbt("xqT", [P, D], BF16)
    p_sb = sbt("p_sb", [P, S], F32)
    p_bf = sbt("p_bf", [P, S], BF16)
    pt_sb = sbt("pt_sb", [P, 512], BF16)
    kcache = sbt("kcache", [KD, S], BF16)
    vcache = sbt("vcache", [P, NT * KD], BF16)
    qT_sb = sbt("qT_sb", [KD, P], BF16)
    oT_sb = sbt("oT_sb", [KD, P], BF16)
    hqT = sbt("hqT", [P, II], BF16)
    outsb = [sbt("outsb0", [P, D], F32)]

    PO = [sbt(f"po{i}", [P, IG], F32) for i in range(6)]

    ii_t = PO[0]
    iif = PO[1]
    idf = sbt("idf", [P, P], F32)
    idb = sbt("idb", [P, P], BF16)
    cm = sbt("cm", [P, P], F32)

    bm_t = sbt("bm_t", [P, 96], F32)
    sc_t = sbt("sc_t", [P, 96], F32)
    iv_t = sbt("iv_t", [P, 96], F32)
    ln_t = sbt("ln_t", [P, 16], F32)
    sm_t = sbt("sm_t", [P, 12], F32)

    out_ps = nc.alloc_psum_tensor("out_ps", [P, D], F32)
    psA = nc.alloc_psum_tensor("psA", [P, 1536], F32)
    psB = nc.alloc_psum_tensor("psB", [P, D], F32)
    pt_ps = nc.alloc_psum_tensor("pt_ps", [P, 512], BF16)

    print(f"[build] sbuf bytes remaining/partition: {nc.sbuf_bytes_remaining}")
    print(f"[build] psum banks remaining: {nc.psum_banks_remaining}")

    b_wqb, b_wkb, b_wv1b, b_wv2b = Buf("wqb"), Buf("wkb"), Buf("wv1b"), Buf("wv2b")
    b_wgT, b_wuT, b_wdT = Buf("wgT"), Buf("wuT"), Buf("wdT")
    b_xb = [Buf("xb0"), Buf("xb1")]
    b_xn, b_xnT, b_xqT = Buf("xn"), Buf("xnT"), Buf("xqT")
    b_psb, b_pbf = Buf("p_sb"), Buf("p_bf")
    b_pts = [Buf(f"pt_ps{i}") for i in range(4)]
    b_ptsb = [Buf(f"pt_sb{i}") for i in range(4)]
    b_kc, b_vc, b_qT, b_oT = Buf("kc"), Buf("vc"), Buf("qT"), Buf("oT")
    b_hqT = Buf("hqT")
    b_outsb = [Buf("outsb0")]
    b_po = [Buf(f"po{i}") for i in range(6)]
    b_idf, b_idb, b_cm = Buf("idf"), Buf("idb"), Buf("cm")
    b_bm, b_sc, b_iv, b_ln, b_sm = Buf("bm"), Buf("sc"), Buf("iv"), Buf("ln"), Buf("sm")
    b_out0, b_out1 = Buf("out0"), Buf("out1")
    b_A = [Buf("A0"), Buf("A1"), Buf("A2")]
    b_B = [Buf("B0"), Buf("B1")]

    dma = lambda fn, reads=(), writes=(), deps=(): pl.op("sync", fn, reads, writes, deps)
    vop = lambda fn, reads=(), writes=(), deps=(): pl.op("vector", fn, reads, writes, deps)
    aop = lambda fn, reads=(), writes=(), deps=(): pl.op("scalar", fn, reads, writes, deps)
    top = lambda fn, reads=(), writes=(), deps=(): pl.op("tensor", fn, reads, writes, deps)
    gop = lambda fn, reads=(), writes=(), deps=(): pl.op("gpsimd", fn, reads, writes, deps)

    gop(lambda g: g.iota(ii_t[:, 0:P].bitcast(I32), [[1, P]], channel_multiplier=-1),
        writes=[b_po[0]])
    vop(lambda v: v.tensor_copy(iif[:, 0:P], ii_t[:, 0:P].bitcast(I32)),
        reads=[b_po[0]], writes=[b_po[1]])
    vop(lambda v: v.tensor_scalar(idf[:], iif[:, 0:P], 0.0, None, ALU.is_equal),
        reads=[b_po[1]], writes=[b_idf])
    vop(lambda v: v.tensor_copy(idb[:], idf[:]), reads=[b_idf], writes=[b_idb])
    vop(lambda v: v.tensor_scalar(cm[:], iif[:, 0:P], 0.0, NEG, ALU.is_gt, ALU.mult),
        reads=[b_po[1]], writes=[b_cm])

    def attw(dst_sb, b_dst, src_d):
        dma(lambda s: s.dma_start(
            p_sb[:, 0:ECH * KD].rearrange("p (c k) -> p c k", k=KD),
            src_d.rearrange("(c p) k -> p c k", p=P)), writes=[b_psb])
        vop(lambda v: v.tensor_copy(dst_sb[:], p_sb[:, 0:ECH * KD]),
            reads=[b_psb], writes=[b_dst])

    attw(wqb, b_wqb, wq_d)
    attw(wkb, b_wkb, wk_d)
    attw(wv1b, b_wv1b, wv1_d)
    dma(lambda s: s.dma_start(p_sb[0:KD, 0:D], wv2_d), writes=[b_psb])
    vop(lambda v: v.tensor_copy(wv2b[:], p_sb[0:KD, 0:D]),
        reads=[b_psb], writes=[b_wv2b])

    def bc16(t, nblk):
        return bass.AP(t, 0, [[96, P], [1, nblk], [0, 16]])

    def quant(src_sb, src_off, b_src, F, dst_sb, dst_off, b_dst, mid_hook=None,
              abs_sign_done=False):
        nblk = F // 16
        s_ = src_sb[:, src_off:src_off + F]
        d_ = dst_sb[:, dst_off:dst_off + F]
        pA, pB, pC, pD, pE = PO[1], PO[2], PO[3], PO[4], PO[5]
        bA, bB, bC, bD, bE = b_po[1], b_po[2], b_po[3], b_po[4], b_po[5]
        vop(lambda v: v.tensor_reduce(
            bm_t[:, 0:nblk], s_.rearrange("p (b k) -> p b k", k=16),
            AX.X, ALU.max, apply_absolute_value=True),
            reads=[b_src], writes=[b_bm])
        vop(lambda v: v.tensor_scalar(sc_t[:, 0:nblk], bm_t[:, 0:nblk],
                                      1.0 / 6.0, 1e-12, ALU.mult, ALU.max),
            reads=[b_bm], writes=[b_sc])
        vop(lambda v: v.reciprocal(iv_t[:, 0:nblk], sc_t[:, 0:nblk]),
            reads=[b_sc], writes=[b_iv])
        if not abs_sign_done:
            aop(lambda a: a.activation(pA[:, 0:F], s_, AF.Abs),
                reads=[b_src], writes=[bA])
            aop(lambda a: a.activation(pB[:, 0:F], s_, AF.Sign),
                reads=[b_src], writes=[bB])
        vop(lambda v: v.tensor_tensor(
            pC[:, 0:F].rearrange("p (b k) -> p b k", k=16),
            pA[:, 0:F].rearrange("p (b k) -> p b k", k=16),
            bc16(iv_t, nblk), ALU.mult),
            reads=[bA, b_iv], writes=[bC])
        an = pC[:, 0:F]
        if mid_hook is not None:
            mid_hook()
        vop(lambda v: v.tensor_scalar(pE[:, 0:F], an, 2.5, None, ALU.is_gt),
            reads=[bC], writes=[bE])
        vop(lambda v: v.tensor_scalar(pA[:, 0:F], an, 0.25, None, ALU.is_gt),
            reads=[bC], writes=[bA])
        vop(lambda v: v.scalar_tensor_tensor(pD[:, 0:F], an, 0.75, pA[:, 0:F],
                                             ALU.is_gt, ALU.add),
            reads=[bC, bA], writes=[bD])
        vop(lambda v: v.scalar_tensor_tensor(pA[:, 0:F], an, 1.25, pD[:, 0:F],
                                             ALU.is_gt, ALU.add),
            reads=[bC, bD], writes=[bA])
        vop(lambda v: v.scalar_tensor_tensor(pD[:, 0:F], an, 1.75, pA[:, 0:F],
                                             ALU.is_gt, ALU.add),
            reads=[bC, bA], writes=[bD])
        vop(lambda v: v.scalar_tensor_tensor(pA[:, 0:F], an, 3.5, pE[:, 0:F],
                                             ALU.is_gt, ALU.add),
            reads=[bC, bE], writes=[bA])
        vop(lambda v: v.tensor_scalar(pE[:, 0:F], an, 5.0, None, ALU.is_gt),
            reads=[bC], writes=[bE])
        vop(lambda v: v.scalar_tensor_tensor(pC[:, 0:F], pD[:, 0:F], 0.5, pA[:, 0:F],
                                             ALU.mult, ALU.add),
            reads=[bD, bA], writes=[bC])
        vop(lambda v: v.scalar_tensor_tensor(pA[:, 0:F], pE[:, 0:F], 2.0, pC[:, 0:F],
                                             ALU.mult, ALU.add),
            reads=[bE, bC], writes=[bA])
        vop(lambda v: v.tensor_tensor(
            pD[:, 0:F].rearrange("p (b k) -> p b k", k=16),
            pB[:, 0:F].rearrange("p (b k) -> p b k", k=16),
            bc16(sc_t, nblk), ALU.mult),
            reads=[bB, b_sc], writes=[bD])
        vop(lambda v: v.tensor_tensor(d_, pA[:, 0:F], pD[:, 0:F], ALU.mult),
            reads=[bA, bD], writes=[b_dst])

    contrib = nc.dram_tensor("contrib", [NUW * P, IG], BF16, kind="Internal")
    gath = nc.dram_tensor("gath", [NCORES * NUW * P, IG], BF16, kind="Internal")
    b_contrib, b_gath = Buf("contrib"), Buf("gath")

    dma(lambda s: s.dma_start(xb[0][:], x_d[0, 0:P, :]), writes=[b_xb[0]])
    dma(lambda s: s.dma_start(xb[1][:], x_d[0, P:2 * P, :]), writes=[b_xb[1]])

    def plan_ln(xc, b_xc):
        vop(lambda v, xc=xc: v.bn_stats(ln_t[:, 0:6], xc[:, 0:512]),
            reads=[b_xc], writes=[b_ln])
        vop(lambda v, xc=xc: v.bn_stats(ln_t[:, 6:12], xc[:, 512:1024]),
            reads=[b_xc], writes=[b_ln])
        vop(lambda v: v.bn_aggr(ln_t[:, 12:14], ln_t[:, 0:12]),
            reads=[b_ln], writes=[b_ln])
        vop(lambda v: v.tensor_scalar(ln_t[:, 14:15], ln_t[:, 13:14], 1.0, EPS,
                                      ALU.mult, ALU.add),
            reads=[b_ln], writes=[b_ln])
        aop(lambda a: a.activation(ln_t[:, 15:16], ln_t[:, 14:15], AF.Ln),
            reads=[b_ln], writes=[b_ln])
        aop(lambda a: a.activation(ln_t[:, 15:16], ln_t[:, 15:16], AF.Exp,
                                   scale=-0.5),
            reads=[b_ln], writes=[b_ln])
        vop(lambda v: v.scalar_tensor_tensor(ln_t[:, 14:15], ln_t[:, 12:13], -1.0,
                                             ln_t[:, 15:16], ALU.mult, ALU.mult),
            reads=[b_ln], writes=[b_ln])
        vop(lambda v, xc=xc: v.tensor_scalar(xn[:], xc[:], ln_t[:, 15:16],
                                             ln_t[:, 14:15], ALU.mult, ALU.add),
            reads=[b_xc, b_ln], writes=[b_xn])

    plan_ln(xb[0], b_xb[0])

    for k in range(NUW):
        dma(lambda s, k=k: s.dma_start(p_sb[:, 0:IG], wsl_d[k]), writes=[b_psb])
        quant(p_sb, 0, b_psb, IG, PO[0], 0, b_po[0])
        for wave, (u0, nu) in enumerate([(0, 8), (8, 3)]):
            for u in range(u0, u0 + nu):
                top(lambda t_, u=u, u0=u0: t_.transpose(
                    psB[:, (u - u0) * P:(u - u0 + 1) * P],
                    PO[0][:, u * P:(u + 1) * P], idf[:]),
                    reads=[b_po[0], b_idf], writes=[b_B[(u - u0) // 4]])
            aop(lambda a, u0=u0, nu=nu: a.activation(
                p_bf[:, u0 * P:(u0 + nu) * P], psB[:, 0:nu * P], AF.Copy),
                reads=[b_B[0]] + ([b_B[1]] if nu > 4 else []), writes=[b_pbf])
        dma(lambda s, k=k: s.dma_start(contrib[k * P:(k + 1) * P, :],
                                       p_bf[:, 0:IG]),
            reads=[b_pbf], writes=[b_contrib] if k == NUW - 1 else [])

    gop(lambda g: g.collective_compute(
        "AllGather", ALU.bypass, replica_groups=[list(range(NCORES))],
        ins=[contrib.ap().opt()], outs=[gath.ap().opt()]),
        reads=[b_contrib], writes=[b_gath])

    last_gu = {}
    for gu in range(len(UNITS)):
        last_gu[UNITS[gu][0]] = gu
    for gu in range(len(UNITS)):
        kind, args = UNITS[gu]
        row = (gu % NCORES) * (NUW * P) + (gu // NCORES) * P
        if kind == "wg" or kind == "wu":
            r = args
            dstT, b_dstT = (wgT, b_wgT) if kind == "wg" else (wuT, b_wuT)
            dma(lambda s, row=row, r=r, dstT=dstT: s.dma_start(
                bass.AP(dstT, r * P, [[ECH * II, P], [II, ECH], [1, P]]),
                gath[row:row + P, 0:D].rearrange("p (e j) -> p e j", j=P)),
                reads=[b_gath],
                writes=[b_dstT] if gu == last_gu[kind] else [])
        else:
            r, g = args
            dma(lambda s, row=row, r=r, g=g: s.dma_start(
                bass.AP(wdT, (g * 11) * D + r * P, [[WCH * D, P], [D, 11], [1, P]]),
                gath[row:row + P, :].rearrange("p (u j) -> p u j", j=P)),
                reads=[b_gath],
                writes=[b_wdT] if gu == last_gu[kind] else [])

    def combine(op, slots, wa, wb):
        cur = slots[0]
        work = [wa, wb]
        for idx, s in enumerate(slots[1:]):
            dst = work[idx % 2]
            vop(lambda v, dst=dst, cur=cur, s=s: v.tensor_tensor(
                sm_t[:, dst:dst + 1], sm_t[:, cur:cur + 1], sm_t[:, s:s + 1], op),
                reads=[b_sm], writes=[b_sm])
            cur = dst
        return cur

    def plan_resid(rb, rt, rxc, rb_xc):
        ob, b_ob = outsb[0], b_outsb[0]
        vop(lambda v, ob=ob, xc=rxc: v.tensor_tensor(ob[:, 0:512], xc[:, 0:512],
                                                     out_ps[:, 0:512], ALU.add),
            reads=[rb_xc, b_out0], writes=[b_ob])
        vop(lambda v, ob=ob, xc=rxc: v.tensor_tensor(ob[:, 512:1024], xc[:, 512:1024],
                                                     out_ps[:, 512:1024], ALU.add),
            reads=[rb_xc, b_out1], writes=[b_ob])
        dma(lambda s, b=rb, t=rt, ob=ob: s.dma_start(
            out_d[b, t * P:(t + 1) * P, :], ob[:]), reads=[b_ob])

    pending_resid = None

    for tile in range(BPC * NT):
        b, t = divmod(tile, NT)
        xc, b_xc = xb[tile % 2], b_xb[tile % 2]

        aop(lambda a: a.activation(PO[1][:, 0:D], xn[:], AF.Abs),
            reads=[b_xn], writes=[b_po[1]])
        aop(lambda a: a.activation(PO[2][:, 0:D], xn[:], AF.Sign),
            reads=[b_xn], writes=[b_po[2]])

        def dbg_dump(name, ap_fn, rbuf, cast_from=None, cast_shape=None):
            if not (debug and tile == 0):
                return
            if cast_from is not None:
                vop(lambda v: v.tensor_copy(PO[1][0:cast_shape[0], 0:cast_shape[1]],
                                            cast_from), reads=[rbuf], writes=[b_po[1]])
                dma(lambda s: s.dma_start(dbg[name],
                                          PO[1][0:cast_shape[0], 0:cast_shape[1]]),
                    reads=[b_po[1]])
            else:
                dma(lambda s: s.dma_start(dbg[name], ap_fn), reads=[rbuf])

        dbg_dump("dxb", xc[:], b_xc)
        dbg_dump("dxn", xn[:], b_xn)

        if pending_resid is not None:
            plan_resid(*pending_resid)
            pending_resid = None
        if 2 <= tile + 1 < BPC * NT:
            nb, ntt = divmod(tile + 1, NT)
            xnx, b_xnx = xb[(tile + 1) % 2], b_xb[(tile + 1) % 2]
            dma(lambda s, nb=nb, ntt=ntt, xnx=xnx: s.dma_start(
                xnx[:], x_d[nb, ntt * P:(ntt + 1) * P, :]), writes=[b_xnx])


        for e in range(ECH):
            top(lambda t_, e=e: t_.transpose(psB[:, e * P:(e + 1) * P],
                                             xn[:, e * P:(e + 1) * P], idf[:]),
                reads=[b_xn, b_idf], writes=[b_B[e // 4]])
        aop(lambda a: a.activation(xnT[:], psB[:], AF.Copy),
            reads=[b_B[0], b_B[1]], writes=[b_xnT])

        for e in range(ECH):
            top(lambda t_, e=e: t_.matmul(psA[0:KD, 0:P],
                                          wqb[:, e * KD:(e + 1) * KD],
                                          xnT[:, e * P:(e + 1) * P],
                                          start=(e == 0), stop=(e == ECH - 1)),
                reads=[b_wqb, b_xnT], writes=[b_A[0]])
        aop(lambda a: a.activation(qT_sb[:], psA[0:KD, 0:P], AF.Copy),
            reads=[b_A[0]], writes=[b_qT])
        dbg_dump("dqT", None, b_qT, cast_from=qT_sb[:], cast_shape=(KD, P))
        for e in range(ECH):
            top(lambda t_, e=e: t_.matmul(psA[0:KD, 512:512 + P],
                                          wkb[:, e * KD:(e + 1) * KD],
                                          xnT[:, e * P:(e + 1) * P],
                                          start=(e == 0), stop=(e == ECH - 1)),
                reads=[b_wkb, b_xnT], writes=[b_A[1]])
        aop(lambda a, t=t: a.activation(kcache[:, t * P:(t + 1) * P],
                                        psA[0:KD, 512:512 + P], AF.Copy),
            reads=[b_A[1]], writes=[b_kc])
        dbg_dump("dkc", None, b_kc, cast_from=kcache[:, 0:P], cast_shape=(KD, P))
        for e in range(ECH):
            top(lambda t_, e=e: t_.matmul(psA[:, 1024:1024 + KD],
                                          xnT[:, e * P:(e + 1) * P],
                                          wv1b[:, e * KD:(e + 1) * KD],
                                          start=(e == 0), stop=(e == ECH - 1)),
                reads=[b_wv1b, b_xnT], writes=[b_A[2]])
        aop(lambda a, t=t: a.activation(vcache[:, t * KD:(t + 1) * KD],
                                        psA[:, 1024:1024 + KD], AF.Copy),
            reads=[b_A[2]], writes=[b_vc])
        dbg_dump("dvc", None, b_vc, cast_from=vcache[:, 0:KD], cast_shape=(P, KD))

        L = (t + 1) * P
        nch = (L + 511) // 512

        def xq_tp():
            for e in range(ECH):
                top(lambda t_, e=e: t_.transpose(psB[:, e * P:(e + 1) * P],
                                                 PO[0][:, e * P:(e + 1) * P], idf[:]),
                    reads=[b_po[0], b_idf], writes=[b_B[e // 4]])
            aop(lambda a: a.activation(xqT[:], psB[:], AF.Copy),
                reads=[b_B[0], b_B[1]], writes=[b_xqT])

        chunks = []

        def plan_scores():
            for c in range(nch):
                n = min(512, L - c * 512)
                if c < 3:
                    ap = psA[:, c * 512:c * 512 + n]
                    bb = b_A[c]
                else:
                    ap = psB[:, 0:n]
                    bb = b_B[0]
                chunks.append((c, n, ap, bb))
                top(lambda t_, ap=ap, c=c, n=n: t_.matmul(
                    ap, qT_sb[:], kcache[:, c * 512:c * 512 + n],
                    start=True, stop=True),
                    reads=[b_qT, b_kc], writes=[bb])

        def plan_softmax_front():
            cd, wo = divmod(t * P, 512)
            dap, dbb = (psA[:, cd * 512 + wo:cd * 512 + wo + P], b_A[cd]) if cd < 3 \
                else (psB[:, wo:wo + P], b_B[0])
            vop(lambda v, dap=dap: v.tensor_tensor(dap, dap, cm[:], ALU.add),
                reads=[b_cm, dbb], writes=[dbb])
            for (c, n, ap, bb) in chunks:
                vop(lambda v, ap=ap, c=c: v.tensor_reduce(
                    sm_t[:, c:c + 1], ap, AX.X, ALU.max),
                    reads=[bb], writes=[b_sm])
            rmx = combine(ALU.max, list(range(nch)), 4, 10) if nch > 1 else 0
            vop(lambda v, rmx=rmx: v.tensor_scalar(
                sm_t[:, 5:6], sm_t[:, rmx:rmx + 1], -ATT_SCALE, None, ALU.mult),
                reads=[b_sm], writes=[b_sm])
            for (c, n, ap, bb) in chunks:
                aop(lambda a, ap=ap, c=c, n=n: a.activation(
                    p_bf[:, c * 512:c * 512 + n], ap, AF.Exp,
                    scale=ATT_SCALE, bias=sm_t[:, 5:6],
                    accum_out=sm_t[:, 6 + c:7 + c]),
                    reads=[bb, b_sm], writes=[b_pbf, b_sm])

        if nch <= 3:
            plan_scores()
            quant(xn, 0, b_xn, D, PO[0], 0, b_po[0], mid_hook=plan_softmax_front,
                  abs_sign_done=True)
            xq_tp()
        else:
            quant(xn, 0, b_xn, D, PO[0], 0, b_po[0], abs_sign_done=True)
            xq_tp()
            plan_scores()
            plan_softmax_front()
        dbg_dump("dxq", PO[0][:, 0:D], b_po[0])
        dbg_dump("dxqT", None, b_xqT, cast_from=xqT[:], cast_shape=(P, D))

        rtt = combine(ALU.add, list(range(6, 6 + nch)), 10, 4) if nch > 1 else 6
        aop(lambda a, rtt=rtt: a.activation(sm_t[:, 1:2], sm_t[:, rtt:rtt + 1], AF.Ln),
            reads=[b_sm], writes=[b_sm])
        vop(lambda v: v.tensor_tensor(sm_t[:, 0:1], sm_t[:, 5:6], sm_t[:, 1:2],
                                      ALU.subtract),
            reads=[b_sm], writes=[b_sm])
        for (c, n, ap, bb) in chunks:
            aop(lambda a, ap=ap, c=c, n=n: a.activation(
                p_bf[:, c * 512:c * 512 + n], ap, AF.Exp,
                scale=ATT_SCALE, bias=sm_t[:, 0:1]),
                reads=[bb, b_sm], writes=[b_pbf])

        def gumm(wT, bwT, g):
            for e in range(ECH):
                for (c0, cw) in GCH:
                    top(lambda t_, e=e, g=g, c0=c0, cw=cw, wT=wT: t_.matmul(
                        psA[:, c0:c0 + cw], xqT[:, e * P:(e + 1) * P],
                        wT[:, e * II + g * IG + c0:e * II + g * IG + c0 + cw],
                        start=(e == 0), stop=(e == ECH - 1)),
                        reads=[b_xqT, bwT], writes=[b_A[c0 // 512]])

        def gu_chunk(wT, bwT, g, c0, cw):
            bk = b_A[c0 // 512]
            for e in range(ECH):
                top(lambda t_, e=e, g=g, c0=c0, cw=cw, wT=wT: t_.matmul(
                    psA[:, c0:c0 + cw], xqT[:, e * P:(e + 1) * P],
                    wT[:, e * II + g * IG + c0:e * II + g * IG + c0 + cw],
                    start=(e == 0), stop=(e == ECH - 1)),
                    reads=[b_xqT, bwT], writes=[bk])
            return bk

        for (c0, cw) in GCH:
            bk = gu_chunk(wgT, b_wgT, 0, c0, cw)
            aop(lambda a, c0=c0, cw=cw: a.activation(
                PO[0][:, c0:c0 + cw], psA[:, c0:c0 + cw], AF.Silu),
                reads=[bk], writes=[b_po[0]])
        dbg_dump("dgs", PO[0][:, 0:IG], b_po[0])

        if tile + 1 < BPC * NT:
            plan_ln(xb[(tile + 1) % 2], b_xb[(tile + 1) % 2])

        for (c0, cw) in GCH:
            bk = gu_chunk(wuT, b_wuT, 0, c0, cw)
            vop(lambda v, c0=c0, cw=cw: v.tensor_tensor(
                PO[0][:, c0:c0 + cw], PO[0][:, c0:c0 + cw], psA[:, c0:c0 + cw],
                ALU.mult),
                reads=[b_po[0], bk], writes=[b_po[0]])
        dbg_dump("dh", PO[0][:, 0:IG], b_po[0])
        aop(lambda a: a.activation(PO[1][:, 0:IG], PO[0][:, 0:IG], AF.Abs),
            reads=[b_po[0]], writes=[b_po[1]])
        aop(lambda a: a.activation(PO[2][:, 0:IG], PO[0][:, 0:IG], AF.Sign),
            reads=[b_po[0]], writes=[b_po[2]])
        for (c0, cw) in GCH:
            bk = gu_chunk(wgT, b_wgT, 1, c0, cw)
            aop(lambda a, c0=c0, cw=cw: a.activation(
                p_sb[:, c0:c0 + cw], psA[:, c0:c0 + cw], AF.Silu),
                reads=[bk], writes=[b_psb])
        for sx in range(t + 1):
            sl = sx % 4
            top(lambda t_, sx=sx, sl=sl: t_.transpose(
                pt_ps[:, sl * P:(sl + 1) * P], p_bf[:, sx * P:(sx + 1) * P], idb[:]),
                reads=[b_pbf, b_idb], writes=[b_pts[sl]])
            aop(lambda a, sl=sl: a.activation(
                pt_sb[:, sl * P:(sl + 1) * P], pt_ps[:, sl * P:(sl + 1) * P], AF.Copy),
                reads=[b_pts[sl]], writes=[b_ptsb[sl]])
            top(lambda t_, sx=sx, sl=sl, t=t: t_.matmul(
                out_ps[0:KD, 0:P], vcache[:, sx * KD:(sx + 1) * KD],
                pt_sb[:, sl * P:(sl + 1) * P],
                start=(sx == 0), stop=(sx == t)),
                reads=[b_vc, b_ptsb[sl]], writes=[b_out0])
        aop(lambda a: a.activation(oT_sb[:], out_ps[0:KD, 0:P], AF.Copy),
            reads=[b_out0], writes=[b_oT])
        dbg_dump("doT", None, b_oT, cast_from=oT_sb[:], cast_shape=(KD, P))
        top(lambda t_: t_.matmul(out_ps[:, 0:512], oT_sb[:], wv2b[:, 0:512],
                                 start=True, stop=False, skip_group_check=True),
            reads=[b_oT, b_wv2b], writes=[b_out0])
        top(lambda t_: t_.matmul(out_ps[:, 512:1024], oT_sb[:], wv2b[:, 512:1024],
                                 start=True, stop=False, skip_group_check=True),
            reads=[b_oT, b_wv2b], writes=[b_out1])

        quant(PO[0], 0, b_po[0], IG, PO[0], 0, b_po[0], abs_sign_done=True)
        dbg_dump("dhq", PO[0][:, 0:IG], b_po[0])
        for (c0, cw) in GCH:
            bk = gu_chunk(wuT, b_wuT, 1, c0, cw)
            vop(lambda v, c0=c0, cw=cw: v.tensor_tensor(
                p_sb[:, c0:c0 + cw], p_sb[:, c0:c0 + cw], psA[:, c0:c0 + cw],
                ALU.mult),
                reads=[b_psb, bk], writes=[b_psb])

        def hq_block(g, hs, b_hs):
            for (u0, nu) in [(0, 8), (8, 3)]:
                for u in range(u0, u0 + nu):
                    top(lambda t_, u=u, u0=u0, hs=hs: t_.transpose(
                        psB[:, (u - u0) * P:(u - u0 + 1) * P],
                        hs[:, u * P:(u + 1) * P], idf[:]),
                        reads=[b_hs, b_idf], writes=[b_B[(u - u0) // 4]])
                aop(lambda a, g=g, u0=u0, nu=nu: a.activation(
                    hqT[:, (g * 11 + u0) * P:(g * 11 + u0 + nu) * P],
                    psB[:, 0:nu * P], AF.Copy),
                    reads=[b_B[0]] + ([b_B[1]] if nu > 4 else []), writes=[b_hqT])
            for u in range(11):
                slot = g * 11 + u
                last = (g == 1 and u == 10)
                top(lambda t_, slot=slot, last=last: t_.matmul(
                    out_ps[:, 0:512], hqT[:, slot * P:(slot + 1) * P],
                    wdT[:, slot * D:slot * D + 512],
                    start=False, stop=last, skip_group_check=True),
                    reads=[b_hqT, b_wdT], writes=[b_out0])
                top(lambda t_, slot=slot, last=last: t_.matmul(
                    out_ps[:, 512:1024], hqT[:, slot * P:(slot + 1) * P],
                    wdT[:, slot * D + 512:slot * D + 1024],
                    start=False, stop=last, skip_group_check=True),
                    reads=[b_hqT, b_wdT], writes=[b_out1])

        aop(lambda a: a.activation(PO[1][:, 0:IG], p_sb[:, 0:IG], AF.Abs),
            reads=[b_psb], writes=[b_po[1]])
        aop(lambda a: a.activation(PO[2][:, 0:IG], p_sb[:, 0:IG], AF.Sign),
            reads=[b_psb], writes=[b_po[2]])
        hq_block(0, PO[0], b_po[0])
        quant(p_sb, 0, b_psb, IG, p_sb, 0, b_psb, abs_sign_done=True)
        hq_block(1, p_sb, b_psb)

        pending_resid = (b, t, xc, b_xc)

    if pending_resid is not None:
        plan_resid(*pending_resid)
        pending_resid = None

    pl.emit(nc)
    n_ops = {e: len(pl.items[e]) for e in ENGINES}
    print(f"[build] ops per engine: {n_ops}")
    return nc


def kernel(L0_attn_x, L0_attn_w_q, L0_attn_w_k, L0_attn_w_v1, L0_attn_w_v2,
           L0_ffn_w_gate, L0_ffn_w_up, L0_ffn_w_down):
    x = np.ascontiguousarray(np.asarray(L0_attn_x, dtype=np.float32))
    ws = {
        "wq": np.ascontiguousarray(np.asarray(L0_attn_w_q, dtype=np.float32)),
        "wk": np.ascontiguousarray(np.asarray(L0_attn_w_k, dtype=np.float32)),
        "wv1": np.ascontiguousarray(np.asarray(L0_attn_w_v1, dtype=np.float32)),
        "wv2": np.ascontiguousarray(np.asarray(L0_attn_w_v2, dtype=np.float32)),
        "wg": np.ascontiguousarray(np.asarray(L0_ffn_w_gate, dtype=np.float32)),
        "wu": np.ascontiguousarray(np.asarray(L0_ffn_w_up, dtype=np.float32)),
        "wd": np.ascontiguousarray(np.asarray(L0_ffn_w_down, dtype=np.float32)),
    }
    nc = build_program()
    wsls = []
    for c in range(NCORES):
        sl = np.zeros((NUW, P, IG), np.float32)
        mine = UNITS[c::NCORES]
        for k, (kind, args) in enumerate(mine):
            if kind == "wg":
                sl[k, :, 0:D] = ws["wg"][args * P:(args + 1) * P, :]
            elif kind == "wu":
                sl[k, :, 0:D] = ws["wu"][args * P:(args + 1) * P, :]
            else:
                r, g = args
                sl[k] = ws["wd"][r * P:(r + 1) * P, g * IG:(g + 1) * IG]
        wsls.append(np.ascontiguousarray(sl))
    wsm = {k: v for k, v in ws.items() if k in ("wq", "wk", "wv1", "wv2")}
    in_maps = [dict(wsm, x=np.ascontiguousarray(x[c * BPC:(c + 1) * BPC]),
                    wsl=wsls[c])
               for c in range(NCORES)]
    res = run_bass_kernel_spmd(nc, in_maps, core_ids=list(range(NCORES)))
    out = np.concatenate([np.asarray(res.results[c]["out"])
                          for c in range(NCORES)], axis=0)
    return out


if __name__ == "__main__":
    build_program()
    print("build ok")


# version 39
# speedup vs baseline: 1.1651x; 1.0403x over previous
import numpy as np
import concourse.bass as bass
import concourse.mybir as mybir
from concourse.bass_utils import run_bass_kernel_spmd

F32 = mybir.dt.float32
BF16 = mybir.dt.bfloat16
I32 = mybir.dt.int32
AF = mybir.ActivationFunctionType
ALU = mybir.AluOpType
AX = mybir.AxisListType

P = 128
D = 1024
KD = 64
II = 2816
S = 2048
BPC = 4
NT = S // P
NCORES = 8
EPS = 1e-5
ATT_SCALE = KD ** -0.5
NEG = -1.0e9
IG = II // 2
ECH = D // P
WCH = II // P
GCH = [(0, 512), (512, 512), (1024, 384)]

UNITS = ([("wg", r) for r in range(WCH)] + [("wu", r) for r in range(WCH)]
         + [("wd", (r, g)) for r in range(ECH) for g in range(2)])
NUW = (len(UNITS) + NCORES - 1) // NCORES

ENGINES = ("sync", "vector", "scalar", "tensor", "gpsimd")


class Buf:
    __slots__ = ("name", "w", "rs")

    def __init__(self, name):
        self.name = name
        self.w = None
        self.rs = []


class Plan:
    def __init__(self):
        self.items = {e: [] for e in ENGINES}

    def op(self, eng, fn, reads=(), writes=(), deps=()):
        dd = []
        for b in reads:
            if b.w is not None:
                dd.append(b.w)
        for b in writes:
            dd.extend(b.rs)
            if b.w is not None:
                dd.append(b.w)
        dd.extend(deps)
        tok = (eng, len(self.items[eng]))
        self.items[eng].append((fn, dd, len(self.items["sync"])))
        for b in reads:
            b.rs.append(tok)
        for b in writes:
            b.w = tok
            b.rs = []
        return tok

    def emit(self, nc):
        marked = set()
        for e in ENGINES:
            for (_, dd, _ns) in self.items[e]:
                marked.update(dd)
        for i in range(len(self.items["sync"])):
            marked.add(("sync", i))
        incidx = {}
        ninc = {}
        for e in ENGINES:
            c = 0
            for i in range(len(self.items[e])):
                if (e, i) in marked:
                    c += 1
                    incidx[(e, i)] = c
            ninc[e] = c
        sems = {e: nc.alloc_semaphore(f"sem_{e}") for e in ENGINES if ninc[e]}
        plan = self

        with nc.Block() as block:
            def section(eng):
                def body(engine):
                    water = {e: 0 for e in ENGINES}
                    for i, (fn, dd, nsync) in enumerate(plan.items[eng]):
                        for (de, di) in dd:
                            if de == eng and eng == "tensor":
                                continue
                            if de == "sync":
                                val = nsync * 16
                            else:
                                tgt = incidx.get((de, di))
                                if tgt is None:
                                    continue
                                val = tgt
                            if val == 0 or water[de] >= val:
                                continue
                            engine.wait_ge(sems[de], val)
                            water[de] = val
                        inst = fn(engine)
                        if (eng, i) in marked:
                            inst.then_inc(sems[eng], 16 if eng == "sync" else 1)
                    if eng == "sync" and ninc["sync"]:
                        engine.wait_ge(sems["sync"], ninc["sync"] * 16)
                return body

            for eng in ENGINES:
                getattr(block, eng)(section(eng))


def build_program(debug=False):
    nc = bass.Bass("TRN2", target_bir_lowering=False,
                   dynamic_dma_scratch_size=4096)
    pl = Plan()
    dbg = {}
    if debug:
        for nm, shp, dt_ in [("dxn", [P, D], F32), ("dxq", [P, D], F32),
                             ("dgs", [P, IG], F32), ("dh", [P, IG], F32),
                             ("dhq", [P, IG], F32), ("dqT", [KD, P], F32),
                             ("dkc", [KD, P], F32), ("dvc", [P, KD], F32),
                             ("dpb", [P, S], F32), ("doT", [KD, P], F32),
                             ("dxqT", [P, D], F32), ("dhqT", [P, II], F32),
                             ("dxb", [P, D], F32)]:
            dbg[nm] = nc.dram_tensor(nm, shp, dt_, kind="ExternalOutput").ap()

    x_d = nc.dram_tensor("x", [BPC, S, D], F32, kind="ExternalInput").ap()
    wq_d = nc.dram_tensor("wq", [D, KD], F32, kind="ExternalInput").ap()
    wk_d = nc.dram_tensor("wk", [D, KD], F32, kind="ExternalInput").ap()
    wv1_d = nc.dram_tensor("wv1", [D, KD], F32, kind="ExternalInput").ap()
    wv2_d = nc.dram_tensor("wv2", [KD, D], F32, kind="ExternalInput").ap()
    wsl_d = nc.dram_tensor("wsl", [NUW, P, IG], F32, kind="ExternalInput").ap()
    out_d = nc.dram_tensor("out", [BPC, S, D], F32, kind="ExternalOutput").ap()

    sbt = nc.alloc_sbuf_tensor
    wqb = sbt("wqb", [P, ECH * KD], BF16)
    wkb = sbt("wkb", [P, ECH * KD], BF16)
    wv1b = sbt("wv1b", [P, ECH * KD], BF16)
    wv2b = sbt("wv2b", [KD, D], BF16)
    wgT = sbt("wgT", [P, ECH * II], BF16)
    wuT = sbt("wuT", [P, ECH * II], BF16)
    wdT = sbt("wdT", [P, WCH * D], BF16)

    xb = [sbt("xb0", [P, D], F32), sbt("xb1", [P, D], F32)]
    xn = sbt("xn", [P, D], F32)
    xnT = sbt("xnT", [P, D], BF16)
    xqT2 = [sbt("xqT0", [P, D], BF16), sbt("xqT1", [P, D], BF16)]
    p_sb = sbt("p_sb", [P, S], F32)
    p_bf = sbt("p_bf", [P, S], BF16)
    pt_sb = sbt("pt_sb", [P, 512], BF16)
    kcache = sbt("kcache", [KD, S], BF16)
    vcache = sbt("vcache", [P, NT * KD], BF16)
    qT_sb = sbt("qT_sb", [KD, P], BF16)
    oT_sb = sbt("oT_sb", [KD, P], BF16)
    hqT = sbt("hqT", [P, II], BF16)
    outsb = [sbt("outsb0", [P, D], F32)]

    PO = [sbt(f"po{i}", [P, IG], F32) for i in range(6)]

    ii_t = PO[0]
    iif = PO[1]
    idf = sbt("idf", [P, P], F32)
    idb = sbt("idb", [P, P], BF16)
    cm = sbt("cm", [P, P], F32)

    bm_t = sbt("bm_t", [P, 96], F32)
    sc_t = sbt("sc_t", [P, 96], F32)
    iv_t = sbt("iv_t", [P, 96], F32)
    ln_t = sbt("ln_t", [P, 16], F32)
    sm_t = sbt("sm_t", [P, 12], F32)

    out_ps = nc.alloc_psum_tensor("out_ps", [P, D], F32)
    psA = nc.alloc_psum_tensor("psA", [P, 1536], F32)
    psB = nc.alloc_psum_tensor("psB", [P, D], F32)
    pt_ps = nc.alloc_psum_tensor("pt_ps", [P, 512], BF16)

    print(f"[build] sbuf bytes remaining/partition: {nc.sbuf_bytes_remaining}")
    print(f"[build] psum banks remaining: {nc.psum_banks_remaining}")

    b_wqb, b_wkb, b_wv1b, b_wv2b = Buf("wqb"), Buf("wkb"), Buf("wv1b"), Buf("wv2b")
    b_wgT, b_wuT, b_wdT = Buf("wgT"), Buf("wuT"), Buf("wdT")
    b_xb = [Buf("xb0"), Buf("xb1")]
    b_xn, b_xnT = Buf("xn"), Buf("xnT")
    b_xqT2 = [Buf("xqT0"), Buf("xqT1")]
    b_psb, b_pbf = Buf("p_sb"), Buf("p_bf")
    b_pts = [Buf(f"pt_ps{i}") for i in range(4)]
    b_ptsb = [Buf(f"pt_sb{i}") for i in range(4)]
    b_kc, b_vc, b_qT, b_oT = Buf("kc"), Buf("vc"), Buf("qT"), Buf("oT")
    b_hqT = Buf("hqT")
    b_outsb = [Buf("outsb0")]
    b_po = [Buf(f"po{i}") for i in range(6)]
    b_idf, b_idb, b_cm = Buf("idf"), Buf("idb"), Buf("cm")
    b_bm, b_sc, b_iv, b_ln, b_sm = Buf("bm"), Buf("sc"), Buf("iv"), Buf("ln"), Buf("sm")
    b_out0, b_out1 = Buf("out0"), Buf("out1")
    b_A = [Buf("A0"), Buf("A1"), Buf("A2")]
    b_B = [Buf("B0"), Buf("B1")]

    dma = lambda fn, reads=(), writes=(), deps=(): pl.op("sync", fn, reads, writes, deps)
    vop = lambda fn, reads=(), writes=(), deps=(): pl.op("vector", fn, reads, writes, deps)
    aop = lambda fn, reads=(), writes=(), deps=(): pl.op("scalar", fn, reads, writes, deps)
    top = lambda fn, reads=(), writes=(), deps=(): pl.op("tensor", fn, reads, writes, deps)
    gop = lambda fn, reads=(), writes=(), deps=(): pl.op("gpsimd", fn, reads, writes, deps)

    gop(lambda g: g.iota(ii_t[:, 0:P].bitcast(I32), [[1, P]], channel_multiplier=-1),
        writes=[b_po[0]])
    vop(lambda v: v.tensor_copy(iif[:, 0:P], ii_t[:, 0:P].bitcast(I32)),
        reads=[b_po[0]], writes=[b_po[1]])
    vop(lambda v: v.tensor_scalar(idf[:], iif[:, 0:P], 0.0, None, ALU.is_equal),
        reads=[b_po[1]], writes=[b_idf])
    vop(lambda v: v.tensor_copy(idb[:], idf[:]), reads=[b_idf], writes=[b_idb])
    vop(lambda v: v.tensor_scalar(cm[:], iif[:, 0:P], 0.0, NEG, ALU.is_gt, ALU.mult),
        reads=[b_po[1]], writes=[b_cm])

    def attw(dst_sb, b_dst, src_d):
        dma(lambda s: s.dma_start(
            p_sb[:, 0:ECH * KD].rearrange("p (c k) -> p c k", k=KD),
            src_d.rearrange("(c p) k -> p c k", p=P)), writes=[b_psb])
        vop(lambda v: v.tensor_copy(dst_sb[:], p_sb[:, 0:ECH * KD]),
            reads=[b_psb], writes=[b_dst])

    attw(wqb, b_wqb, wq_d)
    attw(wkb, b_wkb, wk_d)
    attw(wv1b, b_wv1b, wv1_d)
    dma(lambda s: s.dma_start(p_sb[0:KD, 0:D], wv2_d), writes=[b_psb])
    vop(lambda v: v.tensor_copy(wv2b[:], p_sb[0:KD, 0:D]),
        reads=[b_psb], writes=[b_wv2b])

    def bc16(t, nblk):
        return bass.AP(t, 0, [[96, P], [1, nblk], [0, 16]])

    def quant(src_sb, src_off, b_src, F, dst_sb, dst_off, b_dst, mid_hook=None,
              abs_sign_done=False):
        nblk = F // 16
        s_ = src_sb[:, src_off:src_off + F]
        d_ = dst_sb[:, dst_off:dst_off + F]
        pA, pB, pC, pD, pE = PO[1], PO[2], PO[3], PO[4], PO[5]
        bA, bB, bC, bD, bE = b_po[1], b_po[2], b_po[3], b_po[4], b_po[5]
        vop(lambda v: v.tensor_reduce(
            bm_t[:, 0:nblk], s_.rearrange("p (b k) -> p b k", k=16),
            AX.X, ALU.max, apply_absolute_value=True),
            reads=[b_src], writes=[b_bm])
        vop(lambda v: v.tensor_scalar(sc_t[:, 0:nblk], bm_t[:, 0:nblk],
                                      1.0 / 6.0, 1e-12, ALU.mult, ALU.max),
            reads=[b_bm], writes=[b_sc])
        vop(lambda v: v.reciprocal(iv_t[:, 0:nblk], sc_t[:, 0:nblk]),
            reads=[b_sc], writes=[b_iv])
        if not abs_sign_done:
            aop(lambda a: a.activation(pA[:, 0:F], s_, AF.Abs),
                reads=[b_src], writes=[bA])
            aop(lambda a: a.activation(pB[:, 0:F], s_, AF.Sign),
                reads=[b_src], writes=[bB])
        vop(lambda v: v.tensor_tensor(
            pC[:, 0:F].rearrange("p (b k) -> p b k", k=16),
            pA[:, 0:F].rearrange("p (b k) -> p b k", k=16),
            bc16(iv_t, nblk), ALU.mult),
            reads=[bA, b_iv], writes=[bC])
        an = pC[:, 0:F]
        if mid_hook is not None:
            mid_hook()
        vop(lambda v: v.tensor_scalar(pE[:, 0:F], an, 2.5, None, ALU.is_gt),
            reads=[bC], writes=[bE])
        vop(lambda v: v.tensor_scalar(pA[:, 0:F], an, 0.25, None, ALU.is_gt),
            reads=[bC], writes=[bA])
        vop(lambda v: v.scalar_tensor_tensor(pD[:, 0:F], an, 0.75, pA[:, 0:F],
                                             ALU.is_gt, ALU.add),
            reads=[bC, bA], writes=[bD])
        vop(lambda v: v.scalar_tensor_tensor(pA[:, 0:F], an, 1.25, pD[:, 0:F],
                                             ALU.is_gt, ALU.add),
            reads=[bC, bD], writes=[bA])
        vop(lambda v: v.scalar_tensor_tensor(pD[:, 0:F], an, 1.75, pA[:, 0:F],
                                             ALU.is_gt, ALU.add),
            reads=[bC, bA], writes=[bD])
        vop(lambda v: v.scalar_tensor_tensor(pA[:, 0:F], an, 3.5, pE[:, 0:F],
                                             ALU.is_gt, ALU.add),
            reads=[bC, bE], writes=[bA])
        vop(lambda v: v.tensor_scalar(pE[:, 0:F], an, 5.0, None, ALU.is_gt),
            reads=[bC], writes=[bE])
        vop(lambda v: v.scalar_tensor_tensor(pC[:, 0:F], pD[:, 0:F], 0.5, pA[:, 0:F],
                                             ALU.mult, ALU.add),
            reads=[bD, bA], writes=[bC])
        vop(lambda v: v.scalar_tensor_tensor(pA[:, 0:F], pE[:, 0:F], 2.0, pC[:, 0:F],
                                             ALU.mult, ALU.add),
            reads=[bE, bC], writes=[bA])
        vop(lambda v: v.tensor_tensor(
            pD[:, 0:F].rearrange("p (b k) -> p b k", k=16),
            pB[:, 0:F].rearrange("p (b k) -> p b k", k=16),
            bc16(sc_t, nblk), ALU.mult),
            reads=[bB, b_sc], writes=[bD])
        vop(lambda v: v.tensor_tensor(d_, pA[:, 0:F], pD[:, 0:F], ALU.mult),
            reads=[bA, bD], writes=[b_dst])

    contrib = nc.dram_tensor("contrib", [NUW * P, IG], BF16, kind="Internal")
    gath = nc.dram_tensor("gath", [NCORES * NUW * P, IG], BF16, kind="Internal")
    b_contrib, b_gath = Buf("contrib"), Buf("gath")

    dma(lambda s: s.dma_start(xb[0][:], x_d[0, 0:P, :]), writes=[b_xb[0]])
    dma(lambda s: s.dma_start(xb[1][:], x_d[0, P:2 * P, :]), writes=[b_xb[1]])

    def plan_ln(xc, b_xc):
        vop(lambda v, xc=xc: v.bn_stats(ln_t[:, 0:6], xc[:, 0:512]),
            reads=[b_xc], writes=[b_ln])
        vop(lambda v, xc=xc: v.bn_stats(ln_t[:, 6:12], xc[:, 512:1024]),
            reads=[b_xc], writes=[b_ln])
        vop(lambda v: v.bn_aggr(ln_t[:, 12:14], ln_t[:, 0:12]),
            reads=[b_ln], writes=[b_ln])
        vop(lambda v: v.tensor_scalar(ln_t[:, 14:15], ln_t[:, 13:14], 1.0, EPS,
                                      ALU.mult, ALU.add),
            reads=[b_ln], writes=[b_ln])
        aop(lambda a: a.activation(ln_t[:, 15:16], ln_t[:, 14:15], AF.Ln),
            reads=[b_ln], writes=[b_ln])
        aop(lambda a: a.activation(ln_t[:, 15:16], ln_t[:, 15:16], AF.Exp,
                                   scale=-0.5),
            reads=[b_ln], writes=[b_ln])
        vop(lambda v: v.scalar_tensor_tensor(ln_t[:, 14:15], ln_t[:, 12:13], -1.0,
                                             ln_t[:, 15:16], ALU.mult, ALU.mult),
            reads=[b_ln], writes=[b_ln])
        vop(lambda v, xc=xc: v.tensor_scalar(xn[:], xc[:], ln_t[:, 15:16],
                                             ln_t[:, 14:15], ALU.mult, ALU.add),
            reads=[b_xc, b_ln], writes=[b_xn])

    def plan_xq_abs():
        aop(lambda a: a.activation(PO[1][:, 0:D], xn[:], AF.Abs),
            reads=[b_xn], writes=[b_po[1]])
        aop(lambda a: a.activation(PO[2][:, 0:D], xn[:], AF.Sign),
            reads=[b_xn], writes=[b_po[2]])

    def plan_xq(nxt, b_nxt):
        quant(xn, 0, b_xn, D, PO[0], 0, b_po[0], abs_sign_done=True)
        for e in range(ECH):
            top(lambda t_, e=e: t_.transpose(psB[:, e * P:(e + 1) * P],
                                             PO[0][:, e * P:(e + 1) * P],
                                             idf[:]),
                reads=[b_po[0], b_idf], writes=[b_B[e // 4]])
        aop(lambda a, nxt=nxt: a.activation(nxt[:], psB[:], AF.Copy),
            reads=[b_B[0], b_B[1]], writes=[b_nxt])

    plan_ln(xb[0], b_xb[0])
    plan_xq_abs()
    plan_xq(xqT2[0], b_xqT2[0])

    for k in range(NUW):
        dma(lambda s, k=k: s.dma_start(p_sb[:, 0:IG], wsl_d[k]), writes=[b_psb])
        quant(p_sb, 0, b_psb, IG, PO[0], 0, b_po[0])
        for wave, (u0, nu) in enumerate([(0, 8), (8, 3)]):
            for u in range(u0, u0 + nu):
                top(lambda t_, u=u, u0=u0: t_.transpose(
                    psB[:, (u - u0) * P:(u - u0 + 1) * P],
                    PO[0][:, u * P:(u + 1) * P], idf[:]),
                    reads=[b_po[0], b_idf], writes=[b_B[(u - u0) // 4]])
            aop(lambda a, u0=u0, nu=nu: a.activation(
                p_bf[:, u0 * P:(u0 + nu) * P], psB[:, 0:nu * P], AF.Copy),
                reads=[b_B[0]] + ([b_B[1]] if nu > 4 else []), writes=[b_pbf])
        dma(lambda s, k=k: s.dma_start(contrib[k * P:(k + 1) * P, :],
                                       p_bf[:, 0:IG]),
            reads=[b_pbf], writes=[b_contrib] if k == NUW - 1 else [])

    gop(lambda g: g.collective_compute(
        "AllGather", ALU.bypass, replica_groups=[list(range(NCORES))],
        ins=[contrib.ap().opt()], outs=[gath.ap().opt()]),
        reads=[b_contrib], writes=[b_gath])

    last_gu = {}
    for gu in range(len(UNITS)):
        last_gu[UNITS[gu][0]] = gu
    for gu in range(len(UNITS)):
        kind, args = UNITS[gu]
        row = (gu % NCORES) * (NUW * P) + (gu // NCORES) * P
        if kind == "wg" or kind == "wu":
            r = args
            dstT, b_dstT = (wgT, b_wgT) if kind == "wg" else (wuT, b_wuT)
            dma(lambda s, row=row, r=r, dstT=dstT: s.dma_start(
                bass.AP(dstT, r * P, [[ECH * II, P], [II, ECH], [1, P]]),
                gath[row:row + P, 0:D].rearrange("p (e j) -> p e j", j=P)),
                reads=[b_gath],
                writes=[b_dstT] if gu == last_gu[kind] else [])
        else:
            r, g = args
            dma(lambda s, row=row, r=r, g=g: s.dma_start(
                bass.AP(wdT, (g * 11) * D + r * P, [[WCH * D, P], [D, 11], [1, P]]),
                gath[row:row + P, :].rearrange("p (u j) -> p u j", j=P)),
                reads=[b_gath],
                writes=[b_wdT] if gu == last_gu[kind] else [])

    def combine(op, slots, wa, wb):
        cur = slots[0]
        work = [wa, wb]
        for idx, s in enumerate(slots[1:]):
            dst = work[idx % 2]
            vop(lambda v, dst=dst, cur=cur, s=s: v.tensor_tensor(
                sm_t[:, dst:dst + 1], sm_t[:, cur:cur + 1], sm_t[:, s:s + 1], op),
                reads=[b_sm], writes=[b_sm])
            cur = dst
        return cur

    def plan_resid(rb, rt, rxc, rb_xc):
        ob, b_ob = outsb[0], b_outsb[0]
        vop(lambda v, ob=ob, xc=rxc: v.tensor_tensor(ob[:, 0:512], xc[:, 0:512],
                                                     out_ps[:, 0:512], ALU.add),
            reads=[rb_xc, b_out0], writes=[b_ob])
        vop(lambda v, ob=ob, xc=rxc: v.tensor_tensor(ob[:, 512:1024], xc[:, 512:1024],
                                                     out_ps[:, 512:1024], ALU.add),
            reads=[rb_xc, b_out1], writes=[b_ob])
        dma(lambda s, b=rb, t=rt, ob=ob: s.dma_start(
            out_d[b, t * P:(t + 1) * P, :], ob[:]), reads=[b_ob])

    pending_resid = None

    for tile in range(BPC * NT):
        b, t = divmod(tile, NT)
        xc, b_xc = xb[tile % 2], b_xb[tile % 2]

        xq_c, b_xq_c = xqT2[tile % 2], b_xqT2[tile % 2]

        def dbg_dump(name, ap_fn, rbuf, cast_from=None, cast_shape=None):
            if not (debug and tile == 0):
                return
            if cast_from is not None:
                vop(lambda v: v.tensor_copy(PO[1][0:cast_shape[0], 0:cast_shape[1]],
                                            cast_from), reads=[rbuf], writes=[b_po[1]])
                dma(lambda s: s.dma_start(dbg[name],
                                          PO[1][0:cast_shape[0], 0:cast_shape[1]]),
                    reads=[b_po[1]])
            else:
                dma(lambda s: s.dma_start(dbg[name], ap_fn), reads=[rbuf])

        dbg_dump("dxb", xc[:], b_xc)
        dbg_dump("dxn", xn[:], b_xn)

        if pending_resid is not None:
            plan_resid(*pending_resid)
            pending_resid = None
        if 2 <= tile + 1 < BPC * NT:
            nb, ntt = divmod(tile + 1, NT)
            xnx, b_xnx = xb[(tile + 1) % 2], b_xb[(tile + 1) % 2]
            dma(lambda s, nb=nb, ntt=ntt, xnx=xnx: s.dma_start(
                xnx[:], x_d[nb, ntt * P:(ntt + 1) * P, :]), writes=[b_xnx])


        for e in range(ECH):
            top(lambda t_, e=e: t_.transpose(psB[:, e * P:(e + 1) * P],
                                             xn[:, e * P:(e + 1) * P], idf[:]),
                reads=[b_xn, b_idf], writes=[b_B[e // 4]])
        aop(lambda a: a.activation(xnT[:], psB[:], AF.Copy),
            reads=[b_B[0], b_B[1]], writes=[b_xnT])

        for e in range(ECH):
            top(lambda t_, e=e: t_.matmul(psA[0:KD, 0:P],
                                          wqb[:, e * KD:(e + 1) * KD],
                                          xnT[:, e * P:(e + 1) * P],
                                          start=(e == 0), stop=(e == ECH - 1)),
                reads=[b_wqb, b_xnT], writes=[b_A[0]])
        aop(lambda a: a.activation(qT_sb[:], psA[0:KD, 0:P], AF.Copy),
            reads=[b_A[0]], writes=[b_qT])
        dbg_dump("dqT", None, b_qT, cast_from=qT_sb[:], cast_shape=(KD, P))
        for e in range(ECH):
            top(lambda t_, e=e: t_.matmul(psA[0:KD, 512:512 + P],
                                          wkb[:, e * KD:(e + 1) * KD],
                                          xnT[:, e * P:(e + 1) * P],
                                          start=(e == 0), stop=(e == ECH - 1)),
                reads=[b_wkb, b_xnT], writes=[b_A[1]])
        aop(lambda a, t=t: a.activation(kcache[:, t * P:(t + 1) * P],
                                        psA[0:KD, 512:512 + P], AF.Copy),
            reads=[b_A[1]], writes=[b_kc])
        dbg_dump("dkc", None, b_kc, cast_from=kcache[:, 0:P], cast_shape=(KD, P))
        for e in range(ECH):
            top(lambda t_, e=e: t_.matmul(psA[:, 1024:1024 + KD],
                                          xnT[:, e * P:(e + 1) * P],
                                          wv1b[:, e * KD:(e + 1) * KD],
                                          start=(e == 0), stop=(e == ECH - 1)),
                reads=[b_wv1b, b_xnT], writes=[b_A[2]])
        aop(lambda a, t=t: a.activation(vcache[:, t * KD:(t + 1) * KD],
                                        psA[:, 1024:1024 + KD], AF.Copy),
            reads=[b_A[2]], writes=[b_vc])
        dbg_dump("dvc", None, b_vc, cast_from=vcache[:, 0:KD], cast_shape=(P, KD))

        L = (t + 1) * P
        nch = (L + 511) // 512

        chunks = []

        def plan_scores():
            for c in range(nch):
                n = min(512, L - c * 512)
                if c < 3:
                    ap = psA[:, c * 512:c * 512 + n]
                    bb = b_A[c]
                else:
                    ap = psB[:, 0:n]
                    bb = b_B[0]
                chunks.append((c, n, ap, bb))
                top(lambda t_, ap=ap, c=c, n=n: t_.matmul(
                    ap, qT_sb[:], kcache[:, c * 512:c * 512 + n],
                    start=True, stop=True),
                    reads=[b_qT, b_kc], writes=[bb])

        def plan_softmax_front():
            cd, wo = divmod(t * P, 512)
            dap, dbb = (psA[:, cd * 512 + wo:cd * 512 + wo + P], b_A[cd]) if cd < 3 \
                else (psB[:, wo:wo + P], b_B[0])
            vop(lambda v, dap=dap: v.tensor_tensor(dap, dap, cm[:], ALU.add),
                reads=[b_cm, dbb], writes=[dbb])
            for (c, n, ap, bb) in chunks:
                vop(lambda v, ap=ap, c=c: v.tensor_reduce(
                    sm_t[:, c:c + 1], ap, AX.X, ALU.max),
                    reads=[bb], writes=[b_sm])
            rmx = combine(ALU.max, list(range(nch)), 4, 10) if nch > 1 else 0
            vop(lambda v, rmx=rmx: v.tensor_scalar(
                sm_t[:, 5:6], sm_t[:, rmx:rmx + 1], -ATT_SCALE, None, ALU.mult),
                reads=[b_sm], writes=[b_sm])
            for (c, n, ap, bb) in chunks:
                aop(lambda a, ap=ap, c=c, n=n: a.activation(
                    p_bf[:, c * 512:c * 512 + n], ap, AF.Exp,
                    scale=ATT_SCALE, bias=sm_t[:, 5:6],
                    accum_out=sm_t[:, 6 + c:7 + c]),
                    reads=[bb, b_sm], writes=[b_pbf, b_sm])

        plan_scores()
        plan_softmax_front()

        rtt = combine(ALU.add, list(range(6, 6 + nch)), 10, 4) if nch > 1 else 6
        aop(lambda a, rtt=rtt: a.activation(sm_t[:, 1:2], sm_t[:, rtt:rtt + 1], AF.Ln),
            reads=[b_sm], writes=[b_sm])
        vop(lambda v: v.tensor_tensor(sm_t[:, 0:1], sm_t[:, 5:6], sm_t[:, 1:2],
                                      ALU.subtract),
            reads=[b_sm], writes=[b_sm])
        for (c, n, ap, bb) in chunks:
            aop(lambda a, ap=ap, c=c, n=n: a.activation(
                p_bf[:, c * 512:c * 512 + n], ap, AF.Exp,
                scale=ATT_SCALE, bias=sm_t[:, 0:1]),
                reads=[bb, b_sm], writes=[b_pbf])

        if tile + 1 < BPC * NT:
            plan_ln(xb[(tile + 1) % 2], b_xb[(tile + 1) % 2])

        def gu_chunk(wT, bwT, g, c0, cw):
            bk = b_A[c0 // 512]
            for e in range(ECH):
                top(lambda t_, e=e, g=g, c0=c0, cw=cw, wT=wT, xq_c=xq_c:
                    t_.matmul(
                        psA[:, c0:c0 + cw], xq_c[:, e * P:(e + 1) * P],
                        wT[:, e * II + g * IG + c0:e * II + g * IG + c0 + cw],
                        start=(e == 0), stop=(e == ECH - 1)),
                    reads=[b_xq_c, bwT], writes=[bk])
            return bk

        for (c0, cw) in GCH:
            bk = gu_chunk(wgT, b_wgT, 0, c0, cw)
            aop(lambda a, c0=c0, cw=cw: a.activation(
                PO[0][:, c0:c0 + cw], psA[:, c0:c0 + cw], AF.Silu),
                reads=[bk], writes=[b_po[0]])
        dbg_dump("dgs", PO[0][:, 0:IG], b_po[0])

        for (c0, cw) in GCH:
            bk = gu_chunk(wuT, b_wuT, 0, c0, cw)
            vop(lambda v, c0=c0, cw=cw: v.tensor_tensor(
                PO[0][:, c0:c0 + cw], PO[0][:, c0:c0 + cw], psA[:, c0:c0 + cw],
                ALU.mult),
                reads=[b_po[0], bk], writes=[b_po[0]])
        dbg_dump("dh", PO[0][:, 0:IG], b_po[0])
        aop(lambda a: a.activation(PO[1][:, 0:IG], PO[0][:, 0:IG], AF.Abs),
            reads=[b_po[0]], writes=[b_po[1]])
        aop(lambda a: a.activation(PO[2][:, 0:IG], PO[0][:, 0:IG], AF.Sign),
            reads=[b_po[0]], writes=[b_po[2]])
        for (c0, cw) in GCH:
            bk = gu_chunk(wgT, b_wgT, 1, c0, cw)
            aop(lambda a, c0=c0, cw=cw: a.activation(
                p_sb[:, c0:c0 + cw], psA[:, c0:c0 + cw], AF.Silu),
                reads=[bk], writes=[b_psb])
        for sx in range(t + 1):
            sl = sx % 4
            top(lambda t_, sx=sx, sl=sl: t_.transpose(
                pt_ps[:, sl * P:(sl + 1) * P], p_bf[:, sx * P:(sx + 1) * P], idb[:]),
                reads=[b_pbf, b_idb], writes=[b_pts[sl]])
            aop(lambda a, sl=sl: a.activation(
                pt_sb[:, sl * P:(sl + 1) * P], pt_ps[:, sl * P:(sl + 1) * P], AF.Copy),
                reads=[b_pts[sl]], writes=[b_ptsb[sl]])
            top(lambda t_, sx=sx, sl=sl, t=t: t_.matmul(
                out_ps[0:KD, 0:P], vcache[:, sx * KD:(sx + 1) * KD],
                pt_sb[:, sl * P:(sl + 1) * P],
                start=(sx == 0), stop=(sx == t)),
                reads=[b_vc, b_ptsb[sl]], writes=[b_out0])
        aop(lambda a: a.activation(oT_sb[:], out_ps[0:KD, 0:P], AF.Copy),
            reads=[b_out0], writes=[b_oT])
        dbg_dump("doT", None, b_oT, cast_from=oT_sb[:], cast_shape=(KD, P))
        top(lambda t_: t_.matmul(out_ps[:, 0:512], oT_sb[:], wv2b[:, 0:512],
                                 start=True, stop=False, skip_group_check=True),
            reads=[b_oT, b_wv2b], writes=[b_out0])
        top(lambda t_: t_.matmul(out_ps[:, 512:1024], oT_sb[:], wv2b[:, 512:1024],
                                 start=True, stop=False, skip_group_check=True),
            reads=[b_oT, b_wv2b], writes=[b_out1])

        quant(PO[0], 0, b_po[0], IG, PO[0], 0, b_po[0], abs_sign_done=True)
        dbg_dump("dhq", PO[0][:, 0:IG], b_po[0])
        for (c0, cw) in GCH:
            bk = gu_chunk(wuT, b_wuT, 1, c0, cw)
            vop(lambda v, c0=c0, cw=cw: v.tensor_tensor(
                p_sb[:, c0:c0 + cw], p_sb[:, c0:c0 + cw], psA[:, c0:c0 + cw],
                ALU.mult),
                reads=[b_psb, bk], writes=[b_psb])

        def hq_block(g, hs, b_hs):
            for (u0, nu) in [(0, 8), (8, 3)]:
                for u in range(u0, u0 + nu):
                    top(lambda t_, u=u, u0=u0, hs=hs: t_.transpose(
                        psB[:, (u - u0) * P:(u - u0 + 1) * P],
                        hs[:, u * P:(u + 1) * P], idf[:]),
                        reads=[b_hs, b_idf], writes=[b_B[(u - u0) // 4]])
                aop(lambda a, g=g, u0=u0, nu=nu: a.activation(
                    hqT[:, (g * 11 + u0) * P:(g * 11 + u0 + nu) * P],
                    psB[:, 0:nu * P], AF.Copy),
                    reads=[b_B[0]] + ([b_B[1]] if nu > 4 else []), writes=[b_hqT])
            for u in range(11):
                slot = g * 11 + u
                last = (g == 1 and u == 10)
                top(lambda t_, slot=slot, last=last: t_.matmul(
                    out_ps[:, 0:512], hqT[:, slot * P:(slot + 1) * P],
                    wdT[:, slot * D:slot * D + 512],
                    start=False, stop=last, skip_group_check=True),
                    reads=[b_hqT, b_wdT], writes=[b_out0])
                top(lambda t_, slot=slot, last=last: t_.matmul(
                    out_ps[:, 512:1024], hqT[:, slot * P:(slot + 1) * P],
                    wdT[:, slot * D + 512:slot * D + 1024],
                    start=False, stop=last, skip_group_check=True),
                    reads=[b_hqT, b_wdT], writes=[b_out1])

        aop(lambda a: a.activation(PO[1][:, 0:IG], p_sb[:, 0:IG], AF.Abs),
            reads=[b_psb], writes=[b_po[1]])
        aop(lambda a: a.activation(PO[2][:, 0:IG], p_sb[:, 0:IG], AF.Sign),
            reads=[b_psb], writes=[b_po[2]])
        hq_block(0, PO[0], b_po[0])
        quant(p_sb, 0, b_psb, IG, p_sb, 0, b_psb, abs_sign_done=True)
        if tile + 1 < BPC * NT:
            plan_xq_abs()
        hq_block(1, p_sb, b_psb)

        if tile + 1 < BPC * NT:
            plan_xq(xqT2[(tile + 1) % 2], b_xqT2[(tile + 1) % 2])

        pending_resid = (b, t, xc, b_xc)

    if pending_resid is not None:
        plan_resid(*pending_resid)
        pending_resid = None

    pl.emit(nc)
    n_ops = {e: len(pl.items[e]) for e in ENGINES}
    print(f"[build] ops per engine: {n_ops}")
    return nc


def kernel(L0_attn_x, L0_attn_w_q, L0_attn_w_k, L0_attn_w_v1, L0_attn_w_v2,
           L0_ffn_w_gate, L0_ffn_w_up, L0_ffn_w_down):
    x = np.ascontiguousarray(np.asarray(L0_attn_x, dtype=np.float32))
    ws = {
        "wq": np.ascontiguousarray(np.asarray(L0_attn_w_q, dtype=np.float32)),
        "wk": np.ascontiguousarray(np.asarray(L0_attn_w_k, dtype=np.float32)),
        "wv1": np.ascontiguousarray(np.asarray(L0_attn_w_v1, dtype=np.float32)),
        "wv2": np.ascontiguousarray(np.asarray(L0_attn_w_v2, dtype=np.float32)),
        "wg": np.ascontiguousarray(np.asarray(L0_ffn_w_gate, dtype=np.float32)),
        "wu": np.ascontiguousarray(np.asarray(L0_ffn_w_up, dtype=np.float32)),
        "wd": np.ascontiguousarray(np.asarray(L0_ffn_w_down, dtype=np.float32)),
    }
    nc = build_program()
    wsls = []
    for c in range(NCORES):
        sl = np.zeros((NUW, P, IG), np.float32)
        mine = UNITS[c::NCORES]
        for k, (kind, args) in enumerate(mine):
            if kind == "wg":
                sl[k, :, 0:D] = ws["wg"][args * P:(args + 1) * P, :]
            elif kind == "wu":
                sl[k, :, 0:D] = ws["wu"][args * P:(args + 1) * P, :]
            else:
                r, g = args
                sl[k] = ws["wd"][r * P:(r + 1) * P, g * IG:(g + 1) * IG]
        wsls.append(np.ascontiguousarray(sl))
    wsm = {k: v for k, v in ws.items() if k in ("wq", "wk", "wv1", "wv2")}
    in_maps = [dict(wsm, x=np.ascontiguousarray(x[c * BPC:(c + 1) * BPC]),
                    wsl=wsls[c])
               for c in range(NCORES)]
    res = run_bass_kernel_spmd(nc, in_maps, core_ids=list(range(NCORES)))
    out = np.concatenate([np.asarray(res.results[c]["out"])
                          for c in range(NCORES)], axis=0)
    return out


if __name__ == "__main__":
    build_program()
    print("build ok")


# version 40
# speedup vs baseline: 1.1686x; 1.0031x over previous
import numpy as np
import concourse.bass as bass
import concourse.mybir as mybir
from concourse.bass_utils import run_bass_kernel_spmd

F32 = mybir.dt.float32
BF16 = mybir.dt.bfloat16
I32 = mybir.dt.int32
AF = mybir.ActivationFunctionType
ALU = mybir.AluOpType
AX = mybir.AxisListType

P = 128
D = 1024
KD = 64
II = 2816
S = 2048
BPC = 4
NT = S // P
NCORES = 8
EPS = 1e-5
ATT_SCALE = KD ** -0.5
NEG = -1.0e9
IG = II // 2
ECH = D // P
WCH = II // P
GCH = [(0, 512), (512, 512), (1024, 384)]

UNITS = ([("wg", r) for r in range(WCH)] + [("wu", r) for r in range(WCH)]
         + [("wd", (r, g)) for r in range(ECH) for g in range(2)])
NUW = (len(UNITS) + NCORES - 1) // NCORES

ENGINES = ("sync", "vector", "scalar", "tensor", "gpsimd")


class Buf:
    __slots__ = ("name", "w", "rs")

    def __init__(self, name):
        self.name = name
        self.w = None
        self.rs = []


class Plan:
    def __init__(self):
        self.items = {e: [] for e in ENGINES}

    def op(self, eng, fn, reads=(), writes=(), deps=()):
        dd = []
        for b in reads:
            if b.w is not None:
                dd.append(b.w)
        for b in writes:
            dd.extend(b.rs)
            if b.w is not None:
                dd.append(b.w)
        dd.extend(deps)
        tok = (eng, len(self.items[eng]))
        self.items[eng].append((fn, dd, len(self.items["sync"])))
        for b in reads:
            b.rs.append(tok)
        for b in writes:
            b.w = tok
            b.rs = []
        return tok

    def emit(self, nc):
        marked = set()
        for e in ENGINES:
            for (_, dd, _ns) in self.items[e]:
                marked.update(dd)
        for i in range(len(self.items["sync"])):
            marked.add(("sync", i))
        incidx = {}
        ninc = {}
        for e in ENGINES:
            c = 0
            for i in range(len(self.items[e])):
                if (e, i) in marked:
                    c += 1
                    incidx[(e, i)] = c
            ninc[e] = c
        sems = {e: nc.alloc_semaphore(f"sem_{e}") for e in ENGINES if ninc[e]}
        plan = self

        with nc.Block() as block:
            def section(eng):
                def body(engine):
                    water = {e: 0 for e in ENGINES}
                    for i, (fn, dd, nsync) in enumerate(plan.items[eng]):
                        for (de, di) in dd:
                            if de == eng and eng == "tensor":
                                continue
                            if de == "sync":
                                val = nsync * 16
                            else:
                                tgt = incidx.get((de, di))
                                if tgt is None:
                                    continue
                                val = tgt
                            if val == 0 or water[de] >= val:
                                continue
                            engine.wait_ge(sems[de], val)
                            water[de] = val
                        inst = fn(engine)
                        if (eng, i) in marked:
                            inst.then_inc(sems[eng], 16 if eng == "sync" else 1)
                    if eng == "sync" and ninc["sync"]:
                        engine.wait_ge(sems["sync"], ninc["sync"] * 16)
                return body

            for eng in ENGINES:
                getattr(block, eng)(section(eng))


def build_program(debug=False):
    nc = bass.Bass("TRN2", target_bir_lowering=False,
                   dynamic_dma_scratch_size=4096)
    pl = Plan()
    dbg = {}
    if debug:
        for nm, shp, dt_ in [("dxn", [P, D], F32), ("dxq", [P, D], F32),
                             ("dgs", [P, IG], F32), ("dh", [P, IG], F32),
                             ("dhq", [P, IG], F32), ("dqT", [KD, P], F32),
                             ("dkc", [KD, P], F32), ("dvc", [P, KD], F32),
                             ("dpb", [P, S], F32), ("doT", [KD, P], F32),
                             ("dxqT", [P, D], F32), ("dhqT", [P, II], F32),
                             ("dxb", [P, D], F32)]:
            dbg[nm] = nc.dram_tensor(nm, shp, dt_, kind="ExternalOutput").ap()

    x_d = nc.dram_tensor("x", [BPC, S, D], F32, kind="ExternalInput").ap()
    wq_d = nc.dram_tensor("wq", [D, KD], F32, kind="ExternalInput").ap()
    wk_d = nc.dram_tensor("wk", [D, KD], F32, kind="ExternalInput").ap()
    wv1_d = nc.dram_tensor("wv1", [D, KD], F32, kind="ExternalInput").ap()
    wv2_d = nc.dram_tensor("wv2", [KD, D], F32, kind="ExternalInput").ap()
    wsl_d = nc.dram_tensor("wsl", [NUW, P, IG], F32, kind="ExternalInput").ap()
    out_d = nc.dram_tensor("out", [BPC, S, D], F32, kind="ExternalOutput").ap()

    sbt = nc.alloc_sbuf_tensor
    wqb = sbt("wqb", [P, ECH * KD], BF16)
    wkb = sbt("wkb", [P, ECH * KD], BF16)
    wv1b = sbt("wv1b", [P, ECH * KD], BF16)
    wv2b = sbt("wv2b", [KD, D], BF16)
    wgT = sbt("wgT", [P, ECH * II], BF16)
    wuT = sbt("wuT", [P, ECH * II], BF16)
    wdT = sbt("wdT", [P, WCH * D], BF16)

    xb = [sbt("xb0", [P, D], F32), sbt("xb1", [P, D], F32)]
    xn = sbt("xn", [P, D], F32)
    xnT = sbt("xnT", [P, D], BF16)
    xqT2 = [sbt("xqT0", [P, D], BF16), sbt("xqT1", [P, D], BF16)]
    p_sb = sbt("p_sb", [P, S], F32)
    p_bf = sbt("p_bf", [P, S], BF16)
    pt_sb = sbt("pt_sb", [P, 512], BF16)
    kcache = sbt("kcache", [KD, S], BF16)
    vcache = sbt("vcache", [P, NT * KD], BF16)
    qT_sb = sbt("qT_sb", [KD, P], BF16)
    oT_sb = sbt("oT_sb", [KD, P], BF16)
    hqT = sbt("hqT", [P, II], BF16)
    outsb = [sbt("outsb0", [P, D], F32)]

    PO = [sbt(f"po{i}", [P, IG], F32) for i in range(6)]

    ii_t = PO[0]
    iif = PO[1]
    idf = sbt("idf", [P, P], F32)
    idb = sbt("idb", [P, P], BF16)
    cm = sbt("cm", [P, P], F32)
    cmb = sbt("cmb", [P, P], BF16)

    bm_t = sbt("bm_t", [P, 96], F32)
    sc_t = sbt("sc_t", [P, 96], F32)
    iv_t = sbt("iv_t", [P, 96], F32)
    ln_t = sbt("ln_t", [P, 16], F32)
    sm_t = sbt("sm_t", [P, 12], F32)

    out_ps = nc.alloc_psum_tensor("out_ps", [P, D], F32)
    psA = nc.alloc_psum_tensor("psA", [P, 1536], F32)
    psB = nc.alloc_psum_tensor("psB", [P, D], F32)
    pt_ps = nc.alloc_psum_tensor("pt_ps", [P, 512], BF16)

    print(f"[build] sbuf bytes remaining/partition: {nc.sbuf_bytes_remaining}")
    print(f"[build] psum banks remaining: {nc.psum_banks_remaining}")

    b_wqb, b_wkb, b_wv1b, b_wv2b = Buf("wqb"), Buf("wkb"), Buf("wv1b"), Buf("wv2b")
    b_wgT, b_wuT, b_wdT = Buf("wgT"), Buf("wuT"), Buf("wdT")
    b_xb = [Buf("xb0"), Buf("xb1")]
    b_xn, b_xnT = Buf("xn"), Buf("xnT")
    b_xqT2 = [Buf("xqT0"), Buf("xqT1")]
    b_psb, b_pbf = Buf("p_sb"), Buf("p_bf")
    b_pts = [Buf(f"pt_ps{i}") for i in range(4)]
    b_ptsb = [Buf(f"pt_sb{i}") for i in range(4)]
    b_kc, b_vc, b_qT, b_oT = Buf("kc"), Buf("vc"), Buf("qT"), Buf("oT")
    b_hqT = Buf("hqT")
    b_outsb = [Buf("outsb0")]
    b_po = [Buf(f"po{i}") for i in range(6)]
    b_idf, b_idb, b_cm = Buf("idf"), Buf("idb"), Buf("cm")
    b_cmb = Buf("cmb")
    b_bm, b_sc, b_iv, b_ln, b_sm = Buf("bm"), Buf("sc"), Buf("iv"), Buf("ln"), Buf("sm")
    b_out0, b_out1 = Buf("out0"), Buf("out1")
    b_A = [Buf("A0"), Buf("A1"), Buf("A2")]
    b_B = [Buf("B0"), Buf("B1")]

    dma = lambda fn, reads=(), writes=(), deps=(): pl.op("sync", fn, reads, writes, deps)
    vop = lambda fn, reads=(), writes=(), deps=(): pl.op("vector", fn, reads, writes, deps)
    aop = lambda fn, reads=(), writes=(), deps=(): pl.op("scalar", fn, reads, writes, deps)
    top = lambda fn, reads=(), writes=(), deps=(): pl.op("tensor", fn, reads, writes, deps)
    gop = lambda fn, reads=(), writes=(), deps=(): pl.op("gpsimd", fn, reads, writes, deps)

    gop(lambda g: g.iota(ii_t[:, 0:P].bitcast(I32), [[1, P]], channel_multiplier=-1),
        writes=[b_po[0]])
    vop(lambda v: v.tensor_copy(iif[:, 0:P], ii_t[:, 0:P].bitcast(I32)),
        reads=[b_po[0]], writes=[b_po[1]])
    vop(lambda v: v.tensor_scalar(idf[:], iif[:, 0:P], 0.0, None, ALU.is_equal),
        reads=[b_po[1]], writes=[b_idf])
    vop(lambda v: v.tensor_copy(idb[:], idf[:]), reads=[b_idf], writes=[b_idb])
    vop(lambda v: v.tensor_scalar(cm[:], iif[:, 0:P], 0.0, NEG, ALU.is_gt, ALU.mult),
        reads=[b_po[1]], writes=[b_cm])
    vop(lambda v: v.tensor_copy(cmb[:], cm[:]), reads=[b_cm], writes=[b_cmb])

    def attw(dst_sb, b_dst, src_d):
        dma(lambda s: s.dma_start(
            p_sb[:, 0:ECH * KD].rearrange("p (c k) -> p c k", k=KD),
            src_d.rearrange("(c p) k -> p c k", p=P)), writes=[b_psb])
        vop(lambda v: v.tensor_copy(dst_sb[:], p_sb[:, 0:ECH * KD]),
            reads=[b_psb], writes=[b_dst])

    attw(wqb, b_wqb, wq_d)
    attw(wkb, b_wkb, wk_d)
    attw(wv1b, b_wv1b, wv1_d)
    dma(lambda s: s.dma_start(p_sb[0:KD, 0:D], wv2_d), writes=[b_psb])
    vop(lambda v: v.tensor_copy(wv2b[:], p_sb[0:KD, 0:D]),
        reads=[b_psb], writes=[b_wv2b])

    def bc16(t, nblk):
        return bass.AP(t, 0, [[96, P], [1, nblk], [0, 16]])

    def quant(src_sb, src_off, b_src, F, dst_sb, dst_off, b_dst, mid_hook=None,
              abs_sign_done=False):
        nblk = F // 16
        s_ = src_sb[:, src_off:src_off + F]
        d_ = dst_sb[:, dst_off:dst_off + F]
        pA, pB, pC, pD, pE = PO[1], PO[2], PO[3], PO[4], PO[5]
        bA, bB, bC, bD, bE = b_po[1], b_po[2], b_po[3], b_po[4], b_po[5]
        vop(lambda v: v.tensor_reduce(
            bm_t[:, 0:nblk], s_.rearrange("p (b k) -> p b k", k=16),
            AX.X, ALU.max, apply_absolute_value=True),
            reads=[b_src], writes=[b_bm])
        vop(lambda v: v.tensor_scalar(sc_t[:, 0:nblk], bm_t[:, 0:nblk],
                                      1.0 / 6.0, 1e-12, ALU.mult, ALU.max),
            reads=[b_bm], writes=[b_sc])
        vop(lambda v: v.reciprocal(iv_t[:, 0:nblk], sc_t[:, 0:nblk]),
            reads=[b_sc], writes=[b_iv])
        if not abs_sign_done:
            aop(lambda a: a.activation(pA[:, 0:F], s_, AF.Abs),
                reads=[b_src], writes=[bA])
            aop(lambda a: a.activation(pB[:, 0:F], s_, AF.Sign),
                reads=[b_src], writes=[bB])
        vop(lambda v: v.tensor_tensor(
            pC[:, 0:F].rearrange("p (b k) -> p b k", k=16),
            pA[:, 0:F].rearrange("p (b k) -> p b k", k=16),
            bc16(iv_t, nblk), ALU.mult),
            reads=[bA, b_iv], writes=[bC])
        an = pC[:, 0:F]
        if mid_hook is not None:
            mid_hook()
        vop(lambda v: v.tensor_scalar(pE[:, 0:F], an, 2.5, None, ALU.is_gt),
            reads=[bC], writes=[bE])
        vop(lambda v: v.tensor_scalar(pA[:, 0:F], an, 0.25, None, ALU.is_gt),
            reads=[bC], writes=[bA])
        vop(lambda v: v.scalar_tensor_tensor(pD[:, 0:F], an, 0.75, pA[:, 0:F],
                                             ALU.is_gt, ALU.add),
            reads=[bC, bA], writes=[bD])
        vop(lambda v: v.scalar_tensor_tensor(pA[:, 0:F], an, 1.25, pD[:, 0:F],
                                             ALU.is_gt, ALU.add),
            reads=[bC, bD], writes=[bA])
        vop(lambda v: v.scalar_tensor_tensor(pD[:, 0:F], an, 1.75, pA[:, 0:F],
                                             ALU.is_gt, ALU.add),
            reads=[bC, bA], writes=[bD])
        vop(lambda v: v.scalar_tensor_tensor(pA[:, 0:F], an, 3.5, pE[:, 0:F],
                                             ALU.is_gt, ALU.add),
            reads=[bC, bE], writes=[bA])
        vop(lambda v: v.tensor_scalar(pE[:, 0:F], an, 5.0, None, ALU.is_gt),
            reads=[bC], writes=[bE])
        vop(lambda v: v.scalar_tensor_tensor(pC[:, 0:F], pD[:, 0:F], 0.5, pA[:, 0:F],
                                             ALU.mult, ALU.add),
            reads=[bD, bA], writes=[bC])
        vop(lambda v: v.scalar_tensor_tensor(pA[:, 0:F], pE[:, 0:F], 2.0, pC[:, 0:F],
                                             ALU.mult, ALU.add),
            reads=[bE, bC], writes=[bA])
        vop(lambda v: v.tensor_tensor(
            pD[:, 0:F].rearrange("p (b k) -> p b k", k=16),
            pB[:, 0:F].rearrange("p (b k) -> p b k", k=16),
            bc16(sc_t, nblk), ALU.mult),
            reads=[bB, b_sc], writes=[bD])
        vop(lambda v: v.tensor_tensor(d_, pA[:, 0:F], pD[:, 0:F], ALU.mult),
            reads=[bA, bD], writes=[b_dst])

    contrib = nc.dram_tensor("contrib", [NUW * P, IG], BF16, kind="Internal")
    gath = nc.dram_tensor("gath", [NCORES * NUW * P, IG], BF16, kind="Internal")
    b_contrib, b_gath = Buf("contrib"), Buf("gath")

    dma(lambda s: s.dma_start(xb[0][:], x_d[0, 0:P, :]), writes=[b_xb[0]])
    dma(lambda s: s.dma_start(xb[1][:], x_d[0, P:2 * P, :]), writes=[b_xb[1]])

    def plan_ln(xc, b_xc):
        vop(lambda v, xc=xc: v.bn_stats(ln_t[:, 0:6], xc[:, 0:512]),
            reads=[b_xc], writes=[b_ln])
        vop(lambda v, xc=xc: v.bn_stats(ln_t[:, 6:12], xc[:, 512:1024]),
            reads=[b_xc], writes=[b_ln])
        vop(lambda v: v.bn_aggr(ln_t[:, 12:14], ln_t[:, 0:12]),
            reads=[b_ln], writes=[b_ln])
        vop(lambda v: v.tensor_scalar(ln_t[:, 14:15], ln_t[:, 13:14], 1.0, EPS,
                                      ALU.mult, ALU.add),
            reads=[b_ln], writes=[b_ln])
        aop(lambda a: a.activation(ln_t[:, 15:16], ln_t[:, 14:15], AF.Ln),
            reads=[b_ln], writes=[b_ln])
        aop(lambda a: a.activation(ln_t[:, 15:16], ln_t[:, 15:16], AF.Exp,
                                   scale=-0.5),
            reads=[b_ln], writes=[b_ln])
        vop(lambda v: v.scalar_tensor_tensor(ln_t[:, 14:15], ln_t[:, 12:13], -1.0,
                                             ln_t[:, 15:16], ALU.mult, ALU.mult),
            reads=[b_ln], writes=[b_ln])
        vop(lambda v, xc=xc: v.tensor_scalar(xn[:], xc[:], ln_t[:, 15:16],
                                             ln_t[:, 14:15], ALU.mult, ALU.add),
            reads=[b_xc, b_ln], writes=[b_xn])

    def plan_xq_abs():
        aop(lambda a: a.activation(PO[1][:, 0:D], xn[:], AF.Abs),
            reads=[b_xn], writes=[b_po[1]])
        aop(lambda a: a.activation(PO[2][:, 0:D], xn[:], AF.Sign),
            reads=[b_xn], writes=[b_po[2]])

    def plan_xq(nxt, b_nxt):
        quant(xn, 0, b_xn, D, PO[0], 0, b_po[0], abs_sign_done=True)
        for e in range(ECH):
            top(lambda t_, e=e: t_.transpose(psB[:, e * P:(e + 1) * P],
                                             PO[0][:, e * P:(e + 1) * P],
                                             idf[:]),
                reads=[b_po[0], b_idf], writes=[b_B[e // 4]])
        aop(lambda a, nxt=nxt: a.activation(nxt[:], psB[:], AF.Copy),
            reads=[b_B[0], b_B[1]], writes=[b_nxt])

    plan_ln(xb[0], b_xb[0])
    plan_xq_abs()
    plan_xq(xqT2[0], b_xqT2[0])

    for k in range(NUW):
        dma(lambda s, k=k: s.dma_start(p_sb[:, 0:IG], wsl_d[k]), writes=[b_psb])
        quant(p_sb, 0, b_psb, IG, PO[0], 0, b_po[0])
        for wave, (u0, nu) in enumerate([(0, 8), (8, 3)]):
            for u in range(u0, u0 + nu):
                top(lambda t_, u=u, u0=u0: t_.transpose(
                    psB[:, (u - u0) * P:(u - u0 + 1) * P],
                    PO[0][:, u * P:(u + 1) * P], idf[:]),
                    reads=[b_po[0], b_idf], writes=[b_B[(u - u0) // 4]])
            aop(lambda a, u0=u0, nu=nu: a.activation(
                p_bf[:, u0 * P:(u0 + nu) * P], psB[:, 0:nu * P], AF.Copy),
                reads=[b_B[0]] + ([b_B[1]] if nu > 4 else []), writes=[b_pbf])
        dma(lambda s, k=k: s.dma_start(contrib[k * P:(k + 1) * P, :],
                                       p_bf[:, 0:IG]),
            reads=[b_pbf], writes=[b_contrib] if k == NUW - 1 else [])

    gop(lambda g: g.collective_compute(
        "AllGather", ALU.bypass, replica_groups=[list(range(NCORES))],
        ins=[contrib.ap().opt()], outs=[gath.ap().opt()]),
        reads=[b_contrib], writes=[b_gath])

    last_gu = {}
    for gu in range(len(UNITS)):
        last_gu[UNITS[gu][0]] = gu
    for gu in range(len(UNITS)):
        kind, args = UNITS[gu]
        row = (gu % NCORES) * (NUW * P) + (gu // NCORES) * P
        if kind == "wg" or kind == "wu":
            r = args
            dstT, b_dstT = (wgT, b_wgT) if kind == "wg" else (wuT, b_wuT)
            dma(lambda s, row=row, r=r, dstT=dstT: s.dma_start(
                bass.AP(dstT, r * P, [[ECH * II, P], [II, ECH], [1, P]]),
                gath[row:row + P, 0:D].rearrange("p (e j) -> p e j", j=P)),
                reads=[b_gath],
                writes=[b_dstT] if gu == last_gu[kind] else [])
        else:
            r, g = args
            dma(lambda s, row=row, r=r, g=g: s.dma_start(
                bass.AP(wdT, (g * 11) * D + r * P, [[WCH * D, P], [D, 11], [1, P]]),
                gath[row:row + P, :].rearrange("p (u j) -> p u j", j=P)),
                reads=[b_gath],
                writes=[b_wdT] if gu == last_gu[kind] else [])

    def combine(op, slots, wa, wb):
        cur = slots[0]
        work = [wa, wb]
        for idx, s in enumerate(slots[1:]):
            dst = work[idx % 2]
            vop(lambda v, dst=dst, cur=cur, s=s: v.tensor_tensor(
                sm_t[:, dst:dst + 1], sm_t[:, cur:cur + 1], sm_t[:, s:s + 1], op),
                reads=[b_sm], writes=[b_sm])
            cur = dst
        return cur

    def plan_resid(rb, rt, rxc, rb_xc):
        ob, b_ob = outsb[0], b_outsb[0]
        vop(lambda v, ob=ob, xc=rxc: v.tensor_tensor(ob[:, 0:512], xc[:, 0:512],
                                                     out_ps[:, 0:512], ALU.add),
            reads=[rb_xc, b_out0], writes=[b_ob])
        vop(lambda v, ob=ob, xc=rxc: v.tensor_tensor(ob[:, 512:1024], xc[:, 512:1024],
                                                     out_ps[:, 512:1024], ALU.add),
            reads=[rb_xc, b_out1], writes=[b_ob])
        dma(lambda s, b=rb, t=rt, ob=ob: s.dma_start(
            out_d[b, t * P:(t + 1) * P, :], ob[:]), reads=[b_ob])

    pending_resid = None

    for tile in range(BPC * NT):
        b, t = divmod(tile, NT)
        xc, b_xc = xb[tile % 2], b_xb[tile % 2]

        xq_c, b_xq_c = xqT2[tile % 2], b_xqT2[tile % 2]

        def dbg_dump(name, ap_fn, rbuf, cast_from=None, cast_shape=None):
            if not (debug and tile == 0):
                return
            if cast_from is not None:
                vop(lambda v: v.tensor_copy(PO[1][0:cast_shape[0], 0:cast_shape[1]],
                                            cast_from), reads=[rbuf], writes=[b_po[1]])
                dma(lambda s: s.dma_start(dbg[name],
                                          PO[1][0:cast_shape[0], 0:cast_shape[1]]),
                    reads=[b_po[1]])
            else:
                dma(lambda s: s.dma_start(dbg[name], ap_fn), reads=[rbuf])

        dbg_dump("dxb", xc[:], b_xc)
        dbg_dump("dxn", xn[:], b_xn)

        if pending_resid is not None:
            plan_resid(*pending_resid)
            pending_resid = None
        if 2 <= tile + 1 < BPC * NT:
            nb, ntt = divmod(tile + 1, NT)
            xnx, b_xnx = xb[(tile + 1) % 2], b_xb[(tile + 1) % 2]
            dma(lambda s, nb=nb, ntt=ntt, xnx=xnx: s.dma_start(
                xnx[:], x_d[nb, ntt * P:(ntt + 1) * P, :]), writes=[b_xnx])


        for e in range(ECH):
            top(lambda t_, e=e: t_.transpose(psB[:, e * P:(e + 1) * P],
                                             xn[:, e * P:(e + 1) * P], idf[:]),
                reads=[b_xn, b_idf], writes=[b_B[e // 4]])
        aop(lambda a: a.activation(xnT[:], psB[:], AF.Copy),
            reads=[b_B[0], b_B[1]], writes=[b_xnT])

        for e in range(ECH):
            top(lambda t_, e=e: t_.matmul(psA[0:KD, 0:P],
                                          wqb[:, e * KD:(e + 1) * KD],
                                          xnT[:, e * P:(e + 1) * P],
                                          start=(e == 0), stop=(e == ECH - 1)),
                reads=[b_wqb, b_xnT], writes=[b_A[0]])
        aop(lambda a: a.activation(qT_sb[:], psA[0:KD, 0:P], AF.Copy),
            reads=[b_A[0]], writes=[b_qT])
        dbg_dump("dqT", None, b_qT, cast_from=qT_sb[:], cast_shape=(KD, P))
        for e in range(ECH):
            top(lambda t_, e=e: t_.matmul(psA[0:KD, 512:512 + P],
                                          wkb[:, e * KD:(e + 1) * KD],
                                          xnT[:, e * P:(e + 1) * P],
                                          start=(e == 0), stop=(e == ECH - 1)),
                reads=[b_wkb, b_xnT], writes=[b_A[1]])
        aop(lambda a, t=t: a.activation(kcache[:, t * P:(t + 1) * P],
                                        psA[0:KD, 512:512 + P], AF.Copy),
            reads=[b_A[1]], writes=[b_kc])
        dbg_dump("dkc", None, b_kc, cast_from=kcache[:, 0:P], cast_shape=(KD, P))
        for e in range(ECH):
            top(lambda t_, e=e: t_.matmul(psA[:, 1024:1024 + KD],
                                          xnT[:, e * P:(e + 1) * P],
                                          wv1b[:, e * KD:(e + 1) * KD],
                                          start=(e == 0), stop=(e == ECH - 1)),
                reads=[b_wv1b, b_xnT], writes=[b_A[2]])
        aop(lambda a, t=t: a.activation(vcache[:, t * KD:(t + 1) * KD],
                                        psA[:, 1024:1024 + KD], AF.Copy),
            reads=[b_A[2]], writes=[b_vc])
        dbg_dump("dvc", None, b_vc, cast_from=vcache[:, 0:KD], cast_shape=(P, KD))

        L = (t + 1) * P
        nch = (L + 511) // 512

        chunks = []

        def plan_scores():
            for c in range(nch):
                n = min(512, L - c * 512)
                if c < 3:
                    ap = psA[:, c * 512:c * 512 + n]
                    bb = b_A[c]
                else:
                    ap = psB[:, 0:n]
                    bb = b_B[0]
                chunks.append((c, n, ap, bb))
                top(lambda t_, ap=ap, c=c, n=n: t_.matmul(
                    ap, qT_sb[:], kcache[:, c * 512:c * 512 + n],
                    start=True, stop=True),
                    reads=[b_qT, b_kc], writes=[bb])

        def plan_softmax_front():
            cd, wo = divmod(t * P, 512)
            dap, dbb = (psA[:, cd * 512 + wo:cd * 512 + wo + P], b_A[cd]) if cd < 3 \
                else (psB[:, wo:wo + P], b_B[0])
            top(lambda t_, dap=dap: t_.matmul(dap, idb[:], cmb[:], start=False,
                                              stop=True, skip_group_check=True),
                reads=[b_idb, b_cmb], writes=[dbb])
            for (c, n, ap, bb) in chunks:
                vop(lambda v, ap=ap, c=c: v.tensor_reduce(
                    sm_t[:, c:c + 1], ap, AX.X, ALU.max),
                    reads=[bb], writes=[b_sm])
            rmx = combine(ALU.max, list(range(nch)), 4, 10) if nch > 1 else 0
            vop(lambda v, rmx=rmx: v.tensor_scalar(
                sm_t[:, 5:6], sm_t[:, rmx:rmx + 1], -ATT_SCALE, None, ALU.mult),
                reads=[b_sm], writes=[b_sm])
            for (c, n, ap, bb) in chunks:
                aop(lambda a, ap=ap, c=c, n=n: a.activation(
                    p_bf[:, c * 512:c * 512 + n], ap, AF.Exp,
                    scale=ATT_SCALE, bias=sm_t[:, 5:6],
                    accum_out=sm_t[:, 6 + c:7 + c]),
                    reads=[bb, b_sm], writes=[b_pbf, b_sm])

        plan_scores()
        plan_softmax_front()

        rtt = combine(ALU.add, list(range(6, 6 + nch)), 10, 4) if nch > 1 else 6
        aop(lambda a, rtt=rtt: a.activation(sm_t[:, 1:2], sm_t[:, rtt:rtt + 1], AF.Ln),
            reads=[b_sm], writes=[b_sm])
        vop(lambda v: v.tensor_tensor(sm_t[:, 0:1], sm_t[:, 5:6], sm_t[:, 1:2],
                                      ALU.subtract),
            reads=[b_sm], writes=[b_sm])
        for (c, n, ap, bb) in chunks:
            aop(lambda a, ap=ap, c=c, n=n: a.activation(
                p_bf[:, c * 512:c * 512 + n], ap, AF.Exp,
                scale=ATT_SCALE, bias=sm_t[:, 0:1]),
                reads=[bb, b_sm], writes=[b_pbf])

        if tile + 1 < BPC * NT:
            plan_ln(xb[(tile + 1) % 2], b_xb[(tile + 1) % 2])

        def gu_chunk(wT, bwT, g, c0, cw):
            bk = b_A[c0 // 512]
            for e in range(ECH):
                top(lambda t_, e=e, g=g, c0=c0, cw=cw, wT=wT, xq_c=xq_c:
                    t_.matmul(
                        psA[:, c0:c0 + cw], xq_c[:, e * P:(e + 1) * P],
                        wT[:, e * II + g * IG + c0:e * II + g * IG + c0 + cw],
                        start=(e == 0), stop=(e == ECH - 1)),
                    reads=[b_xq_c, bwT], writes=[bk])
            return bk

        for (c0, cw) in GCH:
            bk = gu_chunk(wgT, b_wgT, 0, c0, cw)
            aop(lambda a, c0=c0, cw=cw: a.activation(
                PO[0][:, c0:c0 + cw], psA[:, c0:c0 + cw], AF.Silu),
                reads=[bk], writes=[b_po[0]])
        dbg_dump("dgs", PO[0][:, 0:IG], b_po[0])

        for (c0, cw) in GCH:
            bk = gu_chunk(wuT, b_wuT, 0, c0, cw)
            vop(lambda v, c0=c0, cw=cw: v.tensor_tensor(
                PO[0][:, c0:c0 + cw], PO[0][:, c0:c0 + cw], psA[:, c0:c0 + cw],
                ALU.mult),
                reads=[b_po[0], bk], writes=[b_po[0]])
        dbg_dump("dh", PO[0][:, 0:IG], b_po[0])
        aop(lambda a: a.activation(PO[1][:, 0:IG], PO[0][:, 0:IG], AF.Abs),
            reads=[b_po[0]], writes=[b_po[1]])
        aop(lambda a: a.activation(PO[2][:, 0:IG], PO[0][:, 0:IG], AF.Sign),
            reads=[b_po[0]], writes=[b_po[2]])
        for (c0, cw) in GCH:
            bk = gu_chunk(wgT, b_wgT, 1, c0, cw)
            aop(lambda a, c0=c0, cw=cw: a.activation(
                p_sb[:, c0:c0 + cw], psA[:, c0:c0 + cw], AF.Silu),
                reads=[bk], writes=[b_psb])
        for sx in range(t + 1):
            sl = sx % 4
            top(lambda t_, sx=sx, sl=sl: t_.transpose(
                pt_ps[:, sl * P:(sl + 1) * P], p_bf[:, sx * P:(sx + 1) * P], idb[:]),
                reads=[b_pbf, b_idb], writes=[b_pts[sl]])
            aop(lambda a, sl=sl: a.activation(
                pt_sb[:, sl * P:(sl + 1) * P], pt_ps[:, sl * P:(sl + 1) * P], AF.Copy),
                reads=[b_pts[sl]], writes=[b_ptsb[sl]])
            top(lambda t_, sx=sx, sl=sl, t=t: t_.matmul(
                out_ps[0:KD, 0:P], vcache[:, sx * KD:(sx + 1) * KD],
                pt_sb[:, sl * P:(sl + 1) * P],
                start=(sx == 0), stop=(sx == t)),
                reads=[b_vc, b_ptsb[sl]], writes=[b_out0])
        aop(lambda a: a.activation(oT_sb[:], out_ps[0:KD, 0:P], AF.Copy),
            reads=[b_out0], writes=[b_oT])
        dbg_dump("doT", None, b_oT, cast_from=oT_sb[:], cast_shape=(KD, P))
        top(lambda t_: t_.matmul(out_ps[:, 0:512], oT_sb[:], wv2b[:, 0:512],
                                 start=True, stop=False, skip_group_check=True),
            reads=[b_oT, b_wv2b], writes=[b_out0])
        top(lambda t_: t_.matmul(out_ps[:, 512:1024], oT_sb[:], wv2b[:, 512:1024],
                                 start=True, stop=False, skip_group_check=True),
            reads=[b_oT, b_wv2b], writes=[b_out1])

        quant(PO[0], 0, b_po[0], IG, PO[0], 0, b_po[0], abs_sign_done=True)
        dbg_dump("dhq", PO[0][:, 0:IG], b_po[0])
        for (c0, cw) in GCH:
            bk = gu_chunk(wuT, b_wuT, 1, c0, cw)
            vop(lambda v, c0=c0, cw=cw: v.tensor_tensor(
                p_sb[:, c0:c0 + cw], p_sb[:, c0:c0 + cw], psA[:, c0:c0 + cw],
                ALU.mult),
                reads=[b_psb, bk], writes=[b_psb])

        def hq_block(g, hs, b_hs):
            for (u0, nu) in [(0, 8), (8, 3)]:
                for u in range(u0, u0 + nu):
                    top(lambda t_, u=u, u0=u0, hs=hs: t_.transpose(
                        psB[:, (u - u0) * P:(u - u0 + 1) * P],
                        hs[:, u * P:(u + 1) * P], idf[:]),
                        reads=[b_hs, b_idf], writes=[b_B[(u - u0) // 4]])
                aop(lambda a, g=g, u0=u0, nu=nu: a.activation(
                    hqT[:, (g * 11 + u0) * P:(g * 11 + u0 + nu) * P],
                    psB[:, 0:nu * P], AF.Copy),
                    reads=[b_B[0]] + ([b_B[1]] if nu > 4 else []), writes=[b_hqT])
            for u in range(11):
                slot = g * 11 + u
                last = (g == 1 and u == 10)
                top(lambda t_, slot=slot, last=last: t_.matmul(
                    out_ps[:, 0:512], hqT[:, slot * P:(slot + 1) * P],
                    wdT[:, slot * D:slot * D + 512],
                    start=False, stop=last, skip_group_check=True),
                    reads=[b_hqT, b_wdT], writes=[b_out0])
                top(lambda t_, slot=slot, last=last: t_.matmul(
                    out_ps[:, 512:1024], hqT[:, slot * P:(slot + 1) * P],
                    wdT[:, slot * D + 512:slot * D + 1024],
                    start=False, stop=last, skip_group_check=True),
                    reads=[b_hqT, b_wdT], writes=[b_out1])

        aop(lambda a: a.activation(PO[1][:, 0:IG], p_sb[:, 0:IG], AF.Abs),
            reads=[b_psb], writes=[b_po[1]])
        aop(lambda a: a.activation(PO[2][:, 0:IG], p_sb[:, 0:IG], AF.Sign),
            reads=[b_psb], writes=[b_po[2]])
        hq_block(0, PO[0], b_po[0])
        quant(p_sb, 0, b_psb, IG, p_sb, 0, b_psb, abs_sign_done=True)
        if tile + 1 < BPC * NT:
            plan_xq_abs()
        hq_block(1, p_sb, b_psb)

        if tile + 1 < BPC * NT:
            plan_xq(xqT2[(tile + 1) % 2], b_xqT2[(tile + 1) % 2])

        pending_resid = (b, t, xc, b_xc)

    if pending_resid is not None:
        plan_resid(*pending_resid)
        pending_resid = None

    pl.emit(nc)
    n_ops = {e: len(pl.items[e]) for e in ENGINES}
    print(f"[build] ops per engine: {n_ops}")
    return nc


def kernel(L0_attn_x, L0_attn_w_q, L0_attn_w_k, L0_attn_w_v1, L0_attn_w_v2,
           L0_ffn_w_gate, L0_ffn_w_up, L0_ffn_w_down):
    x = np.ascontiguousarray(np.asarray(L0_attn_x, dtype=np.float32))
    ws = {
        "wq": np.ascontiguousarray(np.asarray(L0_attn_w_q, dtype=np.float32)),
        "wk": np.ascontiguousarray(np.asarray(L0_attn_w_k, dtype=np.float32)),
        "wv1": np.ascontiguousarray(np.asarray(L0_attn_w_v1, dtype=np.float32)),
        "wv2": np.ascontiguousarray(np.asarray(L0_attn_w_v2, dtype=np.float32)),
        "wg": np.ascontiguousarray(np.asarray(L0_ffn_w_gate, dtype=np.float32)),
        "wu": np.ascontiguousarray(np.asarray(L0_ffn_w_up, dtype=np.float32)),
        "wd": np.ascontiguousarray(np.asarray(L0_ffn_w_down, dtype=np.float32)),
    }
    nc = build_program()
    wsls = []
    for c in range(NCORES):
        sl = np.zeros((NUW, P, IG), np.float32)
        mine = UNITS[c::NCORES]
        for k, (kind, args) in enumerate(mine):
            if kind == "wg":
                sl[k, :, 0:D] = ws["wg"][args * P:(args + 1) * P, :]
            elif kind == "wu":
                sl[k, :, 0:D] = ws["wu"][args * P:(args + 1) * P, :]
            else:
                r, g = args
                sl[k] = ws["wd"][r * P:(r + 1) * P, g * IG:(g + 1) * IG]
        wsls.append(np.ascontiguousarray(sl))
    wsm = {k: v for k, v in ws.items() if k in ("wq", "wk", "wv1", "wv2")}
    in_maps = [dict(wsm, x=np.ascontiguousarray(x[c * BPC:(c + 1) * BPC]),
                    wsl=wsls[c])
               for c in range(NCORES)]
    res = run_bass_kernel_spmd(nc, in_maps, core_ids=list(range(NCORES)))
    out = np.concatenate([np.asarray(res.results[c]["out"])
                          for c in range(NCORES)], axis=0)
    return out


if __name__ == "__main__":
    build_program()
    print("build ok")


# version 41 (final)
# speedup vs baseline: 1.2161x; 1.0406x over previous
import numpy as np
import concourse.bass as bass
import concourse.mybir as mybir
from concourse.bass_utils import run_bass_kernel_spmd

F32 = mybir.dt.float32
BF16 = mybir.dt.bfloat16
I32 = mybir.dt.int32
AF = mybir.ActivationFunctionType
ALU = mybir.AluOpType
AX = mybir.AxisListType

P = 128
D = 1024
KD = 64
II = 2816
S = 2048
BPC = 4
NT = S // P
NCORES = 8
EPS = 1e-5
ATT_SCALE = KD ** -0.5
NEG = -1.0e9
IG = II // 2
ECH = D // P
WCH = II // P
GCH = [(0, 512), (512, 512), (1024, 384)]

UNITS = ([("wg", r) for r in range(WCH)] + [("wu", r) for r in range(WCH)]
         + [("wd", (r, g)) for r in range(ECH) for g in range(2)])
NUW = (len(UNITS) + NCORES - 1) // NCORES

ENGINES = ("sync", "vector", "scalar", "tensor", "gpsimd")


class Buf:
    __slots__ = ("name", "w", "rs")

    def __init__(self, name):
        self.name = name
        self.w = None
        self.rs = []


class Plan:
    def __init__(self):
        self.items = {e: [] for e in ENGINES}

    def op(self, eng, fn, reads=(), writes=(), deps=()):
        dd = []
        for b in reads:
            if b.w is not None:
                dd.append(b.w)
        for b in writes:
            dd.extend(b.rs)
            if b.w is not None:
                dd.append(b.w)
        dd.extend(deps)
        tok = (eng, len(self.items[eng]))
        self.items[eng].append((fn, dd, len(self.items["sync"])))
        for b in reads:
            b.rs.append(tok)
        for b in writes:
            b.w = tok
            b.rs = []
        return tok

    def emit(self, nc):
        marked = set()
        for e in ENGINES:
            for (_, dd, _ns) in self.items[e]:
                marked.update(dd)
        for i in range(len(self.items["sync"])):
            marked.add(("sync", i))
        incidx = {}
        ninc = {}
        for e in ENGINES:
            c = 0
            for i in range(len(self.items[e])):
                if (e, i) in marked:
                    c += 1
                    incidx[(e, i)] = c
            ninc[e] = c
        sems = {e: nc.alloc_semaphore(f"sem_{e}") for e in ENGINES if ninc[e]}
        plan = self

        with nc.Block() as block:
            def section(eng):
                def body(engine):
                    water = {e: 0 for e in ENGINES}
                    for i, (fn, dd, nsync) in enumerate(plan.items[eng]):
                        for (de, di) in dd:
                            if de == eng and eng == "tensor":
                                continue
                            if de == "sync":
                                val = nsync * 16
                            else:
                                tgt = incidx.get((de, di))
                                if tgt is None:
                                    continue
                                val = tgt
                            if val == 0 or water[de] >= val:
                                continue
                            engine.wait_ge(sems[de], val)
                            water[de] = val
                        inst = fn(engine)
                        if (eng, i) in marked:
                            inst.then_inc(sems[eng], 16 if eng == "sync" else 1)
                    if eng == "sync" and ninc["sync"]:
                        engine.wait_ge(sems["sync"], ninc["sync"] * 16)
                return body

            for eng in ENGINES:
                getattr(block, eng)(section(eng))


def build_program(debug=False):
    nc = bass.Bass("TRN2", target_bir_lowering=False,
                   dynamic_dma_scratch_size=4096)
    pl = Plan()
    dbg = {}
    if debug:
        for nm, shp, dt_ in [("dxn", [P, D], F32), ("dxq", [P, D], F32),
                             ("dgs", [P, IG], F32), ("dh", [P, IG], F32),
                             ("dhq", [P, IG], F32), ("dqT", [KD, P], F32),
                             ("dkc", [KD, P], F32), ("dvc", [P, KD], F32),
                             ("dpb", [P, S], F32), ("doT", [KD, P], F32),
                             ("dxqT", [P, D], F32), ("dhqT", [P, II], F32),
                             ("dxb", [P, D], F32)]:
            dbg[nm] = nc.dram_tensor(nm, shp, dt_, kind="ExternalOutput").ap()

    x_d = nc.dram_tensor("x", [BPC, S, D], F32, kind="ExternalInput").ap()
    wq_d = nc.dram_tensor("wq", [D, KD], F32, kind="ExternalInput").ap()
    wk_d = nc.dram_tensor("wk", [D, KD], F32, kind="ExternalInput").ap()
    wv1_d = nc.dram_tensor("wv1", [D, KD], F32, kind="ExternalInput").ap()
    wv2_d = nc.dram_tensor("wv2", [KD, D], F32, kind="ExternalInput").ap()
    wsl_d = nc.dram_tensor("wsl", [NUW, P, IG], F32, kind="ExternalInput").ap()
    out_d = nc.dram_tensor("out", [BPC, S, D], F32, kind="ExternalOutput").ap()

    sbt = nc.alloc_sbuf_tensor
    wqb = sbt("wqb", [P, ECH * KD], BF16)
    wkb = sbt("wkb", [P, ECH * KD], BF16)
    wv1b = sbt("wv1b", [P, ECH * KD], BF16)
    wv2b = sbt("wv2b", [KD, D], BF16)
    wgT = sbt("wgT", [P, ECH * II], BF16)
    wuT = sbt("wuT", [P, ECH * II], BF16)
    wdT = sbt("wdT", [P, WCH * D], BF16)

    xb = [sbt("xb0", [P, D], F32), sbt("xb1", [P, D], F32)]
    xn = sbt("xn", [P, D], F32)
    xnT = sbt("xnT", [P, D], BF16)
    xqT2 = [sbt("xqT0", [P, D], BF16), sbt("xqT1", [P, D], BF16)]
    p_sb = sbt("p_sb", [P, S], F32)
    p_bf = sbt("p_bf", [P, S], BF16)
    pt_sb = sbt("pt_sb", [P, 512], BF16)
    kcache = sbt("kcache", [KD, S], BF16)
    vcache = sbt("vcache", [P, NT * KD], BF16)
    qT_sb = sbt("qT_sb", [KD, P], BF16)
    oT_sb = sbt("oT_sb", [KD, P], BF16)
    hqT = sbt("hqT", [P, II], BF16)
    outsb = [sbt("outsb0", [P, D], F32)]

    PO = [sbt(f"po{i}", [P, IG], F32) for i in range(6)]

    ii_t = PO[0]
    iif = PO[1]
    idf = sbt("idf", [P, P], F32)
    idb = sbt("idb", [P, P], BF16)
    cm = sbt("cm", [P, P], F32)
    cmb = sbt("cmb", [P, P], BF16)

    bm_t = sbt("bm_t", [P, 96], F32)
    sc_t = sbt("sc_t", [P, 96], F32)
    iv_t = sbt("iv_t", [P, 96], F32)
    ln_t = sbt("ln_t", [P, 16], F32)
    sm_t = sbt("sm_t", [P, 12], F32)

    out_ps = nc.alloc_psum_tensor("out_ps", [P, D], F32)
    psA = nc.alloc_psum_tensor("psA", [P, 1536], F32)
    psB = nc.alloc_psum_tensor("psB", [P, D], F32)
    pt_ps = nc.alloc_psum_tensor("pt_ps", [P, 512], BF16)

    print(f"[build] sbuf bytes remaining/partition: {nc.sbuf_bytes_remaining}")
    print(f"[build] psum banks remaining: {nc.psum_banks_remaining}")

    b_wqb, b_wkb, b_wv1b, b_wv2b = Buf("wqb"), Buf("wkb"), Buf("wv1b"), Buf("wv2b")
    b_wgT, b_wuT, b_wdT = Buf("wgT"), Buf("wuT"), Buf("wdT")
    b_xb = [Buf("xb0"), Buf("xb1")]
    b_xn, b_xnT = Buf("xn"), Buf("xnT")
    b_xqT2 = [Buf("xqT0"), Buf("xqT1")]
    b_psb, b_pbf = Buf("p_sb"), Buf("p_bf")
    b_pts = [Buf(f"pt_ps{i}") for i in range(4)]
    b_ptsb = [Buf(f"pt_sb{i}") for i in range(4)]
    b_kc, b_vc, b_qT, b_oT = Buf("kc"), Buf("vc"), Buf("qT"), Buf("oT")
    b_hqT = Buf("hqT")
    b_outsb = [Buf("outsb0")]
    b_po = [Buf(f"po{i}") for i in range(6)]
    b_idf, b_idb, b_cm = Buf("idf"), Buf("idb"), Buf("cm")
    b_cmb = Buf("cmb")
    b_bm, b_sc, b_iv, b_ln, b_sm = Buf("bm"), Buf("sc"), Buf("iv"), Buf("ln"), Buf("sm")
    b_out0, b_out1 = Buf("out0"), Buf("out1")
    b_A = [Buf("A0"), Buf("A1"), Buf("A2")]
    b_B = [Buf("B0"), Buf("B1")]

    dma = lambda fn, reads=(), writes=(), deps=(): pl.op("sync", fn, reads, writes, deps)
    vop = lambda fn, reads=(), writes=(), deps=(): pl.op("vector", fn, reads, writes, deps)
    aop = lambda fn, reads=(), writes=(), deps=(): pl.op("scalar", fn, reads, writes, deps)
    top = lambda fn, reads=(), writes=(), deps=(): pl.op("tensor", fn, reads, writes, deps)
    gop = lambda fn, reads=(), writes=(), deps=(): pl.op("gpsimd", fn, reads, writes, deps)

    gop(lambda g: g.iota(ii_t[:, 0:P].bitcast(I32), [[1, P]], channel_multiplier=-1),
        writes=[b_po[0]])
    vop(lambda v: v.tensor_copy(iif[:, 0:P], ii_t[:, 0:P].bitcast(I32)),
        reads=[b_po[0]], writes=[b_po[1]])
    vop(lambda v: v.tensor_scalar(idf[:], iif[:, 0:P], 0.0, None, ALU.is_equal),
        reads=[b_po[1]], writes=[b_idf])
    vop(lambda v: v.tensor_copy(idb[:], idf[:]), reads=[b_idf], writes=[b_idb])
    vop(lambda v: v.tensor_scalar(cm[:], iif[:, 0:P], 0.0, NEG, ALU.is_gt, ALU.mult),
        reads=[b_po[1]], writes=[b_cm])
    vop(lambda v: v.tensor_copy(cmb[:], cm[:]), reads=[b_cm], writes=[b_cmb])

    def attw(dst_sb, b_dst, src_d):
        dma(lambda s: s.dma_start(
            p_sb[:, 0:ECH * KD].rearrange("p (c k) -> p c k", k=KD),
            src_d.rearrange("(c p) k -> p c k", p=P)), writes=[b_psb])
        vop(lambda v: v.tensor_copy(dst_sb[:], p_sb[:, 0:ECH * KD]),
            reads=[b_psb], writes=[b_dst])

    attw(wqb, b_wqb, wq_d)
    attw(wkb, b_wkb, wk_d)
    attw(wv1b, b_wv1b, wv1_d)
    dma(lambda s: s.dma_start(p_sb[0:KD, 0:D], wv2_d), writes=[b_psb])
    vop(lambda v: v.tensor_copy(wv2b[:], p_sb[0:KD, 0:D]),
        reads=[b_psb], writes=[b_wv2b])

    def bc16(t, nblk):
        return bass.AP(t, 0, [[96, P], [1, nblk], [0, 16]])

    def quant(src_sb, src_off, b_src, F, dst_sb, dst_off, b_dst, mid_hook=None,
              abs_sign_done=False):
        nblk = F // 16
        s_ = src_sb[:, src_off:src_off + F]
        d_ = dst_sb[:, dst_off:dst_off + F]
        pA, pB, pC, pD, pE = PO[1], PO[2], PO[3], PO[4], PO[5]
        bA, bB, bC, bD, bE = b_po[1], b_po[2], b_po[3], b_po[4], b_po[5]
        vop(lambda v: v.tensor_reduce(
            bm_t[:, 0:nblk], s_.rearrange("p (b k) -> p b k", k=16),
            AX.X, ALU.max, apply_absolute_value=True),
            reads=[b_src], writes=[b_bm])
        vop(lambda v: v.tensor_scalar(sc_t[:, 0:nblk], bm_t[:, 0:nblk],
                                      1.0 / 6.0, 1e-12, ALU.mult, ALU.max),
            reads=[b_bm], writes=[b_sc])
        vop(lambda v: v.reciprocal(iv_t[:, 0:nblk], sc_t[:, 0:nblk]),
            reads=[b_sc], writes=[b_iv])
        if not abs_sign_done:
            aop(lambda a: a.activation(pA[:, 0:F], s_, AF.Abs),
                reads=[b_src], writes=[bA])
            aop(lambda a: a.activation(pB[:, 0:F], s_, AF.Sign),
                reads=[b_src], writes=[bB])
        vop(lambda v: v.tensor_tensor(
            pC[:, 0:F].rearrange("p (b k) -> p b k", k=16),
            pA[:, 0:F].rearrange("p (b k) -> p b k", k=16),
            bc16(iv_t, nblk), ALU.mult),
            reads=[bA, b_iv], writes=[bC])
        an = pC[:, 0:F]
        if mid_hook is not None:
            mid_hook()
        vop(lambda v: v.tensor_scalar(pE[:, 0:F], an, 2.5, None, ALU.is_gt),
            reads=[bC], writes=[bE])
        vop(lambda v: v.tensor_scalar(pA[:, 0:F], an, 0.25, None, ALU.is_gt),
            reads=[bC], writes=[bA])
        vop(lambda v: v.scalar_tensor_tensor(pD[:, 0:F], an, 0.75, pA[:, 0:F],
                                             ALU.is_gt, ALU.add),
            reads=[bC, bA], writes=[bD])
        vop(lambda v: v.scalar_tensor_tensor(pA[:, 0:F], an, 1.25, pD[:, 0:F],
                                             ALU.is_gt, ALU.add),
            reads=[bC, bD], writes=[bA])
        vop(lambda v: v.scalar_tensor_tensor(pD[:, 0:F], an, 1.75, pA[:, 0:F],
                                             ALU.is_gt, ALU.add),
            reads=[bC, bA], writes=[bD])
        vop(lambda v: v.scalar_tensor_tensor(pA[:, 0:F], an, 3.5, pE[:, 0:F],
                                             ALU.is_gt, ALU.add),
            reads=[bC, bE], writes=[bA])
        vop(lambda v: v.tensor_scalar(pE[:, 0:F], an, 5.0, None, ALU.is_gt),
            reads=[bC], writes=[bE])
        vop(lambda v: v.scalar_tensor_tensor(pC[:, 0:F], pD[:, 0:F], 0.5, pA[:, 0:F],
                                             ALU.mult, ALU.add),
            reads=[bD, bA], writes=[bC])
        vop(lambda v: v.scalar_tensor_tensor(pA[:, 0:F], pE[:, 0:F], 2.0, pC[:, 0:F],
                                             ALU.mult, ALU.add),
            reads=[bE, bC], writes=[bA])
        vop(lambda v: v.tensor_tensor(
            pD[:, 0:F].rearrange("p (b k) -> p b k", k=16),
            pB[:, 0:F].rearrange("p (b k) -> p b k", k=16),
            bc16(sc_t, nblk), ALU.mult),
            reads=[bB, b_sc], writes=[bD])
        vop(lambda v: v.tensor_tensor(d_, pA[:, 0:F], pD[:, 0:F], ALU.mult),
            reads=[bA, bD], writes=[b_dst])

    contrib = nc.dram_tensor("contrib", [NUW * P, IG], BF16, kind="Internal")
    gath = nc.dram_tensor("gath", [NCORES * NUW * P, IG], BF16, kind="Internal")
    b_contrib, b_gath = Buf("contrib"), Buf("gath")

    dma(lambda s: s.dma_start(xb[0][:], x_d[0, 0:P, :]), writes=[b_xb[0]])
    dma(lambda s: s.dma_start(xb[1][:], x_d[0, P:2 * P, :]), writes=[b_xb[1]])

    def plan_ln(xc, b_xc):
        vop(lambda v, xc=xc: v.bn_stats(ln_t[:, 0:6], xc[:, 0:512]),
            reads=[b_xc], writes=[b_ln])
        vop(lambda v, xc=xc: v.bn_stats(ln_t[:, 6:12], xc[:, 512:1024]),
            reads=[b_xc], writes=[b_ln])
        vop(lambda v: v.bn_aggr(ln_t[:, 12:14], ln_t[:, 0:12]),
            reads=[b_ln], writes=[b_ln])
        vop(lambda v: v.tensor_scalar(ln_t[:, 14:15], ln_t[:, 13:14], 1.0, EPS,
                                      ALU.mult, ALU.add),
            reads=[b_ln], writes=[b_ln])
        aop(lambda a: a.activation(ln_t[:, 15:16], ln_t[:, 14:15], AF.Ln),
            reads=[b_ln], writes=[b_ln])
        aop(lambda a: a.activation(ln_t[:, 15:16], ln_t[:, 15:16], AF.Exp,
                                   scale=-0.5),
            reads=[b_ln], writes=[b_ln])
        vop(lambda v: v.scalar_tensor_tensor(ln_t[:, 14:15], ln_t[:, 12:13], -1.0,
                                             ln_t[:, 15:16], ALU.mult, ALU.mult),
            reads=[b_ln], writes=[b_ln])
        vop(lambda v, xc=xc: v.tensor_scalar(xn[:], xc[:], ln_t[:, 15:16],
                                             ln_t[:, 14:15], ALU.mult, ALU.add),
            reads=[b_xc, b_ln], writes=[b_xn])

    def plan_xq_abs():
        aop(lambda a: a.activation(PO[1][:, 0:D], xn[:], AF.Abs),
            reads=[b_xn], writes=[b_po[1]])
        aop(lambda a: a.activation(PO[2][:, 0:D], xn[:], AF.Sign),
            reads=[b_xn], writes=[b_po[2]])

    def plan_xq(nxt, b_nxt):
        quant(xn, 0, b_xn, D, PO[0], 0, b_po[0], abs_sign_done=True)
        for e in range(ECH):
            top(lambda t_, e=e: t_.transpose(psB[:, e * P:(e + 1) * P],
                                             PO[0][:, e * P:(e + 1) * P],
                                             idf[:]),
                reads=[b_po[0], b_idf], writes=[b_B[e // 4]])
        aop(lambda a, nxt=nxt: a.activation(nxt[:], psB[:], AF.Copy),
            reads=[b_B[0], b_B[1]], writes=[b_nxt])

    plan_ln(xb[0], b_xb[0])
    plan_xq_abs()
    plan_xq(xqT2[0], b_xqT2[0])

    for k in range(NUW):
        dma(lambda s, k=k: s.dma_start(p_sb[:, 0:IG], wsl_d[k]), writes=[b_psb])
        quant(p_sb, 0, b_psb, IG, PO[0], 0, b_po[0])
        for wave, (u0, nu) in enumerate([(0, 8), (8, 3)]):
            for u in range(u0, u0 + nu):
                top(lambda t_, u=u, u0=u0: t_.transpose(
                    psB[:, (u - u0) * P:(u - u0 + 1) * P],
                    PO[0][:, u * P:(u + 1) * P], idf[:]),
                    reads=[b_po[0], b_idf], writes=[b_B[(u - u0) // 4]])
            aop(lambda a, u0=u0, nu=nu: a.activation(
                p_bf[:, u0 * P:(u0 + nu) * P], psB[:, 0:nu * P], AF.Copy),
                reads=[b_B[0]] + ([b_B[1]] if nu > 4 else []), writes=[b_pbf])
        dma(lambda s, k=k: s.dma_start(contrib[k * P:(k + 1) * P, :],
                                       p_bf[:, 0:IG]),
            reads=[b_pbf], writes=[b_contrib] if k == NUW - 1 else [])

    gop(lambda g: g.collective_compute(
        "AllGather", ALU.bypass, replica_groups=[list(range(NCORES))],
        ins=[contrib.ap().opt()], outs=[gath.ap().opt()]),
        reads=[b_contrib], writes=[b_gath])

    last_gu = {}
    for gu in range(len(UNITS)):
        last_gu[UNITS[gu][0]] = gu
    for gu in range(len(UNITS)):
        kind, args = UNITS[gu]
        row = (gu % NCORES) * (NUW * P) + (gu // NCORES) * P
        if kind == "wg" or kind == "wu":
            r = args
            dstT, b_dstT = (wgT, b_wgT) if kind == "wg" else (wuT, b_wuT)
            dma(lambda s, row=row, r=r, dstT=dstT: s.dma_start(
                bass.AP(dstT, r * P, [[ECH * II, P], [II, ECH], [1, P]]),
                gath[row:row + P, 0:D].rearrange("p (e j) -> p e j", j=P)),
                reads=[b_gath],
                writes=[b_dstT] if gu == last_gu[kind] else [])
        else:
            r, g = args
            dma(lambda s, row=row, r=r, g=g: s.dma_start(
                bass.AP(wdT, (g * 11) * D + r * P, [[WCH * D, P], [D, 11], [1, P]]),
                gath[row:row + P, :].rearrange("p (u j) -> p u j", j=P)),
                reads=[b_gath],
                writes=[b_wdT] if gu == last_gu[kind] else [])

    def combine(op, slots, wa, wb):
        cur = slots[0]
        work = [wa, wb]
        for idx, s in enumerate(slots[1:]):
            dst = work[idx % 2]
            vop(lambda v, dst=dst, cur=cur, s=s: v.tensor_tensor(
                sm_t[:, dst:dst + 1], sm_t[:, cur:cur + 1], sm_t[:, s:s + 1], op),
                reads=[b_sm], writes=[b_sm])
            cur = dst
        return cur

    def plan_resid(rb, rt, rxc, rb_xc):
        ob, b_ob = outsb[0], b_outsb[0]
        vop(lambda v, ob=ob, xc=rxc: v.tensor_tensor(ob[:, 0:512], xc[:, 0:512],
                                                     out_ps[:, 0:512], ALU.add),
            reads=[rb_xc, b_out0], writes=[b_ob])
        vop(lambda v, ob=ob, xc=rxc: v.tensor_tensor(ob[:, 512:1024], xc[:, 512:1024],
                                                     out_ps[:, 512:1024], ALU.add),
            reads=[rb_xc, b_out1], writes=[b_ob])
        dma(lambda s, b=rb, t=rt, ob=ob: s.dma_start(
            out_d[b, t * P:(t + 1) * P, :], ob[:]), reads=[b_ob])

    pending_resid = None

    for tile in range(BPC * NT):
        b, t = divmod(tile, NT)
        xc, b_xc = xb[tile % 2], b_xb[tile % 2]

        xq_c, b_xq_c = xqT2[tile % 2], b_xqT2[tile % 2]

        def dbg_dump(name, ap_fn, rbuf, cast_from=None, cast_shape=None):
            if not (debug and tile == 0):
                return
            if cast_from is not None:
                vop(lambda v: v.tensor_copy(PO[1][0:cast_shape[0], 0:cast_shape[1]],
                                            cast_from), reads=[rbuf], writes=[b_po[1]])
                dma(lambda s: s.dma_start(dbg[name],
                                          PO[1][0:cast_shape[0], 0:cast_shape[1]]),
                    reads=[b_po[1]])
            else:
                dma(lambda s: s.dma_start(dbg[name], ap_fn), reads=[rbuf])

        dbg_dump("dxb", xc[:], b_xc)
        dbg_dump("dxn", xn[:], b_xn)

        if pending_resid is not None:
            plan_resid(*pending_resid)
            pending_resid = None
        if 2 <= tile + 1 < BPC * NT:
            nb, ntt = divmod(tile + 1, NT)
            xnx, b_xnx = xb[(tile + 1) % 2], b_xb[(tile + 1) % 2]
            dma(lambda s, nb=nb, ntt=ntt, xnx=xnx: s.dma_start(
                xnx[:], x_d[nb, ntt * P:(ntt + 1) * P, :]), writes=[b_xnx])


        for e in range(ECH):
            top(lambda t_, e=e: t_.transpose(psB[:, e * P:(e + 1) * P],
                                             xn[:, e * P:(e + 1) * P], idf[:]),
                reads=[b_xn, b_idf], writes=[b_B[e // 4]])
        aop(lambda a: a.activation(xnT[:], psB[:], AF.Copy),
            reads=[b_B[0], b_B[1]], writes=[b_xnT])

        for e in range(ECH):
            top(lambda t_, e=e: t_.matmul(psA[0:KD, 0:P],
                                          wqb[:, e * KD:(e + 1) * KD],
                                          xnT[:, e * P:(e + 1) * P],
                                          start=(e == 0), stop=(e == ECH - 1)),
                reads=[b_wqb, b_xnT], writes=[b_A[0]])
        aop(lambda a: a.activation(qT_sb[:], psA[0:KD, 0:P], AF.Copy),
            reads=[b_A[0]], writes=[b_qT])
        dbg_dump("dqT", None, b_qT, cast_from=qT_sb[:], cast_shape=(KD, P))
        for e in range(ECH):
            top(lambda t_, e=e: t_.matmul(psA[0:KD, 512:512 + P],
                                          wkb[:, e * KD:(e + 1) * KD],
                                          xnT[:, e * P:(e + 1) * P],
                                          start=(e == 0), stop=(e == ECH - 1)),
                reads=[b_wkb, b_xnT], writes=[b_A[1]])
        aop(lambda a, t=t: a.activation(kcache[:, t * P:(t + 1) * P],
                                        psA[0:KD, 512:512 + P], AF.Copy),
            reads=[b_A[1]], writes=[b_kc])
        dbg_dump("dkc", None, b_kc, cast_from=kcache[:, 0:P], cast_shape=(KD, P))
        for e in range(ECH):
            top(lambda t_, e=e: t_.matmul(psA[:, 1024:1024 + KD],
                                          xnT[:, e * P:(e + 1) * P],
                                          wv1b[:, e * KD:(e + 1) * KD],
                                          start=(e == 0), stop=(e == ECH - 1)),
                reads=[b_wv1b, b_xnT], writes=[b_A[2]])
        aop(lambda a, t=t: a.activation(vcache[:, t * KD:(t + 1) * KD],
                                        psA[:, 1024:1024 + KD], AF.Copy),
            reads=[b_A[2]], writes=[b_vc])
        dbg_dump("dvc", None, b_vc, cast_from=vcache[:, 0:KD], cast_shape=(P, KD))

        L = (t + 1) * P
        nch = (L + 511) // 512

        chunks = []

        def plan_scores():
            for c in range(nch):
                n = min(512, L - c * 512)
                if c < 3:
                    ap = psA[:, c * 512:c * 512 + n]
                    bb = b_A[c]
                else:
                    ap = psB[:, 0:n]
                    bb = b_B[0]
                chunks.append((c, n, ap, bb))
                top(lambda t_, ap=ap, c=c, n=n: t_.matmul(
                    ap, qT_sb[:], kcache[:, c * 512:c * 512 + n],
                    start=True, stop=True),
                    reads=[b_qT, b_kc], writes=[bb])

        def plan_softmax_front():
            cd, wo = divmod(t * P, 512)
            dap, dbb = (psA[:, cd * 512 + wo:cd * 512 + wo + P], b_A[cd]) if cd < 3 \
                else (psB[:, wo:wo + P], b_B[0])
            top(lambda t_, dap=dap: t_.matmul(dap, idb[:], cmb[:], start=False,
                                              stop=True, skip_group_check=True),
                reads=[b_idb, b_cmb], writes=[dbb])
            for (c, n, ap, bb) in chunks:
                vop(lambda v, ap=ap, c=c: v.tensor_reduce(
                    sm_t[:, c:c + 1], ap, AX.X, ALU.max),
                    reads=[bb], writes=[b_sm])
            rmx = combine(ALU.max, list(range(nch)), 4, 10) if nch > 1 else 0
            vop(lambda v, rmx=rmx: v.tensor_scalar(
                sm_t[:, 5:6], sm_t[:, rmx:rmx + 1], -ATT_SCALE, None, ALU.mult),
                reads=[b_sm], writes=[b_sm])
            for (c, n, ap, bb) in chunks:
                aop(lambda a, ap=ap, c=c, n=n: a.activation(
                    p_bf[:, c * 512:c * 512 + n], ap, AF.Exp,
                    scale=ATT_SCALE, bias=sm_t[:, 5:6],
                    accum_out=sm_t[:, 6 + c:7 + c]),
                    reads=[bb, b_sm], writes=[b_pbf, b_sm])

        plan_scores()
        plan_softmax_front()

        rtt = combine(ALU.add, list(range(6, 6 + nch)), 10, 4) if nch > 1 else 6
        aop(lambda a, rtt=rtt: a.activation(sm_t[:, 1:2], sm_t[:, rtt:rtt + 1], AF.Ln),
            reads=[b_sm], writes=[b_sm])
        vop(lambda v: v.tensor_tensor(sm_t[:, 0:1], sm_t[:, 5:6], sm_t[:, 1:2],
                                      ALU.subtract),
            reads=[b_sm], writes=[b_sm])
        for (c, n, ap, bb) in chunks:
            aop(lambda a, ap=ap, c=c, n=n: a.activation(
                p_bf[:, c * 512:c * 512 + n], ap, AF.Exp,
                scale=ATT_SCALE, bias=sm_t[:, 0:1]),
                reads=[bb, b_sm], writes=[b_pbf])

        if tile + 1 < BPC * NT:
            plan_ln(xb[(tile + 1) % 2], b_xb[(tile + 1) % 2])

        def gu_chunk(wT, bwT, g, c0, cw):
            bk = b_A[c0 // 512]
            for e in range(ECH):
                top(lambda t_, e=e, g=g, c0=c0, cw=cw, wT=wT, xq_c=xq_c:
                    t_.matmul(
                        psA[:, c0:c0 + cw], xq_c[:, e * P:(e + 1) * P],
                        wT[:, e * II + g * IG + c0:e * II + g * IG + c0 + cw],
                        start=(e == 0), stop=(e == ECH - 1)),
                    reads=[b_xq_c, bwT], writes=[bk])
            return bk

        for (c0, cw) in GCH:
            bk = gu_chunk(wgT, b_wgT, 0, c0, cw)
            aop(lambda a, c0=c0, cw=cw: a.activation(
                PO[0][:, c0:c0 + cw], psA[:, c0:c0 + cw], AF.Silu),
                reads=[bk], writes=[b_po[0]])
        dbg_dump("dgs", PO[0][:, 0:IG], b_po[0])

        for (c0, cw) in GCH:
            bk = gu_chunk(wuT, b_wuT, 0, c0, cw)
            vop(lambda v, c0=c0, cw=cw: v.tensor_tensor(
                PO[0][:, c0:c0 + cw], PO[0][:, c0:c0 + cw], psA[:, c0:c0 + cw],
                ALU.mult),
                reads=[b_po[0], bk], writes=[b_po[0]])
        dbg_dump("dh", PO[0][:, 0:IG], b_po[0])
        aop(lambda a: a.activation(PO[1][:, 0:IG], PO[0][:, 0:IG], AF.Abs),
            reads=[b_po[0]], writes=[b_po[1]])
        aop(lambda a: a.activation(PO[2][:, 0:IG], PO[0][:, 0:IG], AF.Sign),
            reads=[b_po[0]], writes=[b_po[2]])
        for (c0, cw) in GCH:
            bk = gu_chunk(wgT, b_wgT, 1, c0, cw)
            aop(lambda a, c0=c0, cw=cw: a.activation(
                p_sb[:, c0:c0 + cw], psA[:, c0:c0 + cw], AF.Silu),
                reads=[bk], writes=[b_psb])
        for w0 in range(0, t + 1, 4):
            wn = min(4, t + 1 - w0)
            for sx in range(w0, w0 + wn):
                sl = sx - w0
                top(lambda t_, sx=sx, sl=sl: t_.transpose(
                    pt_ps[:, sl * P:(sl + 1) * P],
                    p_bf[:, sx * P:(sx + 1) * P], idb[:]),
                    reads=[b_pbf, b_idb], writes=[b_pts[sl]])
            aop(lambda a, wn=wn: a.activation(
                pt_sb[:, 0:wn * P], pt_ps[:, 0:wn * P], AF.Copy),
                reads=[b_pts[sl2] for sl2 in range(wn)],
                writes=[b_ptsb[sl2] for sl2 in range(wn)])
            for sx in range(w0, w0 + wn):
                sl = sx - w0
                top(lambda t_, sx=sx, sl=sl, t=t: t_.matmul(
                    out_ps[0:KD, 0:P], vcache[:, sx * KD:(sx + 1) * KD],
                    pt_sb[:, sl * P:(sl + 1) * P],
                    start=(sx == 0), stop=(sx == t)),
                    reads=[b_vc, b_ptsb[sl]], writes=[b_out0])
        aop(lambda a: a.activation(oT_sb[:], out_ps[0:KD, 0:P], AF.Copy),
            reads=[b_out0], writes=[b_oT])
        dbg_dump("doT", None, b_oT, cast_from=oT_sb[:], cast_shape=(KD, P))
        top(lambda t_: t_.matmul(out_ps[:, 0:512], oT_sb[:], wv2b[:, 0:512],
                                 start=True, stop=False, skip_group_check=True),
            reads=[b_oT, b_wv2b], writes=[b_out0])
        top(lambda t_: t_.matmul(out_ps[:, 512:1024], oT_sb[:], wv2b[:, 512:1024],
                                 start=True, stop=False, skip_group_check=True),
            reads=[b_oT, b_wv2b], writes=[b_out1])

        quant(PO[0], 0, b_po[0], IG, PO[0], 0, b_po[0], abs_sign_done=True)
        dbg_dump("dhq", PO[0][:, 0:IG], b_po[0])
        for (c0, cw) in GCH:
            bk = gu_chunk(wuT, b_wuT, 1, c0, cw)
            vop(lambda v, c0=c0, cw=cw: v.tensor_tensor(
                p_sb[:, c0:c0 + cw], p_sb[:, c0:c0 + cw], psA[:, c0:c0 + cw],
                ALU.mult),
                reads=[b_psb, bk], writes=[b_psb])

        def hq_block(g, hs, b_hs):
            for (u0, nu) in [(0, 8), (8, 3)]:
                for u in range(u0, u0 + nu):
                    top(lambda t_, u=u, u0=u0, hs=hs: t_.transpose(
                        psB[:, (u - u0) * P:(u - u0 + 1) * P],
                        hs[:, u * P:(u + 1) * P], idf[:]),
                        reads=[b_hs, b_idf], writes=[b_B[(u - u0) // 4]])
                aop(lambda a, g=g, u0=u0, nu=nu: a.activation(
                    hqT[:, (g * 11 + u0) * P:(g * 11 + u0 + nu) * P],
                    psB[:, 0:nu * P], AF.Copy),
                    reads=[b_B[0]] + ([b_B[1]] if nu > 4 else []), writes=[b_hqT])
            for u in range(11):
                slot = g * 11 + u
                last = (g == 1 and u == 10)
                top(lambda t_, slot=slot, last=last: t_.matmul(
                    out_ps[:, 0:512], hqT[:, slot * P:(slot + 1) * P],
                    wdT[:, slot * D:slot * D + 512],
                    start=False, stop=last, skip_group_check=True),
                    reads=[b_hqT, b_wdT], writes=[b_out0])
                top(lambda t_, slot=slot, last=last: t_.matmul(
                    out_ps[:, 512:1024], hqT[:, slot * P:(slot + 1) * P],
                    wdT[:, slot * D + 512:slot * D + 1024],
                    start=False, stop=last, skip_group_check=True),
                    reads=[b_hqT, b_wdT], writes=[b_out1])

        aop(lambda a: a.activation(PO[1][:, 0:IG], p_sb[:, 0:IG], AF.Abs),
            reads=[b_psb], writes=[b_po[1]])
        aop(lambda a: a.activation(PO[2][:, 0:IG], p_sb[:, 0:IG], AF.Sign),
            reads=[b_psb], writes=[b_po[2]])
        hq_block(0, PO[0], b_po[0])
        quant(p_sb, 0, b_psb, IG, p_sb, 0, b_psb, abs_sign_done=True)
        if tile + 1 < BPC * NT:
            plan_xq_abs()
        hq_block(1, p_sb, b_psb)

        if tile + 1 < BPC * NT:
            plan_xq(xqT2[(tile + 1) % 2], b_xqT2[(tile + 1) % 2])

        pending_resid = (b, t, xc, b_xc)

    if pending_resid is not None:
        plan_resid(*pending_resid)
        pending_resid = None

    pl.emit(nc)
    n_ops = {e: len(pl.items[e]) for e in ENGINES}
    print(f"[build] ops per engine: {n_ops}")
    return nc


def kernel(L0_attn_x, L0_attn_w_q, L0_attn_w_k, L0_attn_w_v1, L0_attn_w_v2,
           L0_ffn_w_gate, L0_ffn_w_up, L0_ffn_w_down):
    x = np.ascontiguousarray(np.asarray(L0_attn_x, dtype=np.float32))
    ws = {
        "wq": np.ascontiguousarray(np.asarray(L0_attn_w_q, dtype=np.float32)),
        "wk": np.ascontiguousarray(np.asarray(L0_attn_w_k, dtype=np.float32)),
        "wv1": np.ascontiguousarray(np.asarray(L0_attn_w_v1, dtype=np.float32)),
        "wv2": np.ascontiguousarray(np.asarray(L0_attn_w_v2, dtype=np.float32)),
        "wg": np.ascontiguousarray(np.asarray(L0_ffn_w_gate, dtype=np.float32)),
        "wu": np.ascontiguousarray(np.asarray(L0_ffn_w_up, dtype=np.float32)),
        "wd": np.ascontiguousarray(np.asarray(L0_ffn_w_down, dtype=np.float32)),
    }
    nc = build_program()
    wsls = []
    for c in range(NCORES):
        sl = np.zeros((NUW, P, IG), np.float32)
        mine = UNITS[c::NCORES]
        for k, (kind, args) in enumerate(mine):
            if kind == "wg":
                sl[k, :, 0:D] = ws["wg"][args * P:(args + 1) * P, :]
            elif kind == "wu":
                sl[k, :, 0:D] = ws["wu"][args * P:(args + 1) * P, :]
            else:
                r, g = args
                sl[k] = ws["wd"][r * P:(r + 1) * P, g * IG:(g + 1) * IG]
        wsls.append(np.ascontiguousarray(sl))
    wsm = {k: v for k, v in ws.items() if k in ("wq", "wk", "wv1", "wv2")}
    in_maps = [dict(wsm, x=np.ascontiguousarray(x[c * BPC:(c + 1) * BPC]),
                    wsl=wsls[c])
               for c in range(NCORES)]
    res = run_bass_kernel_spmd(nc, in_maps, core_ids=list(range(NCORES)))
    out = np.concatenate([np.asarray(res.results[c]["out"])
                          for c in range(NCORES)], axis=0)
    return out


if __name__ == "__main__":
    build_program()
    print("build ok")
